# version 46 (final)
# baseline (speedup 1.0000x reference)
import contextlib
import numpy as np
import concourse.bass as bass
import concourse.mybir as mybir
from concourse.bass_utils import run_bass_kernel_spmd

F32 = mybir.dt.float32
BF16 = mybir.dt.bfloat16
AF = mybir.ActivationFunctionType
ALU = mybir.AluOpType
AX = mybir.AxisListType

D = 1024
NH = 16
DH = 64
FF = 4096
DC = 8
FC = 32
EPS = 1e-5

ENGINES = ("te", "dv", "ac", "sp", "gp")


class Sched:

    def __init__(self):
        self.ops = []
        self.res = {}
        self.last_dma = {}
        self.dma_cum = {}

    def add(self, eng, fn, reads=(), writes=(), dma=None, extra_deps=()):
        i = len(self.ops)
        deps = set(extra_deps)
        for k in reads:
            e = self.res.get(k)
            if e is not None and e["w"] is not None:
                deps.add(e["w"])
        for k in writes:
            e = self.res.get(k)
            if e is not None:
                if e["w"] is not None:
                    deps.add(e["w"])
                deps.update(e["r"])
        dmaidx = None
        if dma is not None:
            prev = self.last_dma.get(dma)
            if prev is not None:
                deps.add(prev)
            self.last_dma[dma] = i
            dmaidx = self.dma_cum.get(dma, 0) + 1
            self.dma_cum[dma] = dmaidx
        deps.discard(i)
        self.ops.append({
            "i": i, "eng": eng, "fn": fn, "deps": deps, "dma": dma,
            "dmaidx": dmaidx, "needed": False,
        })
        for k in writes:
            self.res[k] = {"w": i, "r": []}
        for k in reads:
            e = self.res.setdefault(k, {"w": None, "r": []})
            e["r"].append(i)
        return i

    def barrier(self, tag):
        last = {}
        for op in self.ops:
            if op["dma"] is None:
                last[op["eng"]] = op["i"]
        deps = tuple(last.values()) + tuple(self.last_dma.values())
        for e in ENGINES:
            self.add(e, lambda eng: eng.nop(), extra_deps=deps)
        self.res = {}

    def emit(self, nc, block, st):
        ops = self.ops
        for op in ops:
            best = {}
            for d in op["deps"]:
                p = ops[d]
                if p["eng"] == "te" and op["eng"] == "te" and p["dma"] is None \
                        and op["dma"] is None:
                    continue
                key = ("D", p["dma"]) if p["dma"] is not None else ("E", p["eng"])
                if key not in best or best[key] < d:
                    best[key] = d
            op["edeps"] = best
        for op in ops:
            for key, d in op["edeps"].items():
                if key[0] == "E":
                    ops[d]["needed"] = True
        cnt = {e: 0 for e in ENGINES}
        for op in ops:
            if op["dma"] is not None:
                continue
            if op["needed"]:
                cnt[op["eng"]] += 1
                op["inc_idx"] = cnt[op["eng"]]
        for op in ops:
            w = []
            for key, d in op["edeps"].items():
                p = ops[d]
                if key[0] == "D":
                    w.append(("D" + str(key[1]), 16 * p["dmaidx"]))
                else:
                    w.append(("E" + p["eng"], p["inc_idx"]))
            op["waits"] = w
        semnames = set("E" + e for e in ENGINES)
        semnames.update("D" + str(k) for k in self.dma_cum)
        sems = {n: st.enter_context(nc.semaphore(n[:60])) for n in sorted(semnames)}
        per_eng = {e: [op for op in ops if op["eng"] == e] for e in ENGINES}

        def mk(engops):
            def run(eng):
                clock = {}
                for op in engops:
                    for sname, val in sorted(op["waits"]):
                        if clock.get(sname, 0) < val:
                            eng.wait_ge(sems[sname], val)
                            clock[sname] = val
                    inst = op["fn"](eng)
                    if op["dma"] is not None:
                        inst.then_inc(sems["D" + str(op["dma"])], 16)
                    elif op["needed"]:
                        inst.then_inc(sems["E" + op["eng"]], 1)
            return run

        block.tensor(mk(per_eng["te"]))
        block.vector(mk(per_eng["dv"]))
        block.scalar(mk(per_eng["ac"]))
        block.sync(mk(per_eng["sp"]))
        block.gpsimd(mk(per_eng["gp"]))


def build_program(T=4096, debug=False):
    NT = T // 128
    NG = T // 512
    nc = bass.Bass("TRN2", target_bir_lowering=False)
    S = Sched()

    d_x = nc.dram_tensor("x", [T, D], F32, kind="ExternalInput").ap()
    d_wqT = nc.dram_tensor("wqT", [D, D], F32, kind="ExternalInput").ap()
    d_wkT = nc.dram_tensor("wkT", [D, D], F32, kind="ExternalInput").ap()
    d_wvT = nc.dram_tensor("wvT", [D, D], F32, kind="ExternalInput").ap()
    d_woT = nc.dram_tensor("woT", [D, D], F32, kind="ExternalInput").ap()
    d_w1T = nc.dram_tensor("w1Tp", [16 * 128, 2048], F32,
                           kind="ExternalInput").ap()
    d_w2T = nc.dram_tensor("w2T", [FF, D], F32, kind="ExternalInput").ap()
    d_bqT = nc.dram_tensor("bqT", [128, 8], F32, kind="ExternalInput").ap()
    d_bkT = nc.dram_tensor("bkT", [128, 8], F32, kind="ExternalInput").ap()
    d_b1T = nc.dram_tensor("b1T", [128, 32], F32, kind="ExternalInput").ap()
    d_bqr = nc.dram_tensor("bqr", [1, D], F32, kind="ExternalInput").ap()
    d_bkr = nc.dram_tensor("bkr", [1, D], F32, kind="ExternalInput").ap()
    d_bvr = nc.dram_tensor("bvr", [1, D], F32, kind="ExternalInput").ap()
    d_bor = nc.dram_tensor("bor", [1, D], F32, kind="ExternalInput").ap()
    d_b2r = nc.dram_tensor("b2r", [1, D], F32, kind="ExternalInput").ap()
    d_ident = nc.dram_tensor("ident", [128, 128], F32, kind="ExternalInput").ap()
    d_tri = nc.dram_tensor("tri", [128, 128], BF16, kind="ExternalInput").ap()
    d_y = nc.dram_tensor("y", [T, D], F32, kind="ExternalOutput").ap()
    d_ots = nc.dram_tensor("OTS", [128, T // 128, 8, 128], BF16).ap()
    dbg = {}

    def dbg_out(name, shape, dtype=F32):
        dbg[name] = nc.dram_tensor("dbg_" + name, shape, dtype,
                                   kind="ExternalOutput").ap()
        return dbg[name]

    st = contextlib.ExitStack()
    with st:
        block = st.enter_context(nc.Block())

        pBA = st.enter_context(nc.psum_tensor("pBA", [128, 1024], F32))
        pBB = st.enter_context(nc.psum_tensor("pBB", [128, 1024], F32))
        pSA = st.enter_context(nc.psum_tensor("pSA", [128, 512], F32))
        pSB = st.enter_context(nc.psum_tensor("pSB", [128, 512], F32))
        pSC = st.enter_context(nc.psum_tensor("pSC", [128, 512], F32))
        pSD = st.enter_context(nc.psum_tensor("pSD", [128, 512], F32))

        s_ident = st.enter_context(nc.sbuf_tensor("s_ident", [128, 128], F32))
        s_tri = st.enter_context(nc.sbuf_tensor("s_tri", [128, 128], BF16))
        s_bqT = st.enter_context(nc.sbuf_tensor("s_bqT", [128, 8], F32))
        s_bkT = st.enter_context(nc.sbuf_tensor("s_bkT", [128, 8], F32))
        s_b1T = st.enter_context(nc.sbuf_tensor("s_b1T", [128, 32], F32))
        s_bv = st.enter_context(nc.sbuf_tensor("s_bv", [1, D], BF16))
        s_bo = st.enter_context(nc.sbuf_tensor("s_bo", [1, D], BF16))
        s_b2 = st.enter_context(nc.sbuf_tensor("s_b2", [1, D], BF16))
        s_ones_r = st.enter_context(nc.sbuf_tensor("s_ones_r", [1, 128], BF16))
        s_ones_c = st.enter_context(nc.sbuf_tensor("s_ones_c", [128, 1], BF16))
        s_ones_rf = st.enter_context(nc.sbuf_tensor("s_ones_rf", [1, 128], F32))
        s_ones_cf = st.enter_context(nc.sbuf_tensor("s_ones_cf", [128, 1], F32))
        s_eps = st.enter_context(nc.sbuf_tensor("s_eps", [128, 1], F32))
        s_gs = st.enter_context(nc.sbuf_tensor("s_gs", [128, 2], F32))
        s_gacc = st.enter_context(nc.sbuf_tensor("s_gacc", [128, 2], F32))

        def dma(eng, out_ap, in_ap, key, reads=(), writes=()):
            S.add(eng, lambda e, o=out_ap, i=in_ap: e.dma_start(o, i),
                  reads=reads, writes=writes, dma=key)

        dma("sp", s_ident[:], d_ident, "ident", writes=[("ident",)])
        dma("sp", s_tri[:], d_tri, "tri", writes=[("tri",)])
        dma("sp", s_bqT[:], d_bqT, "bqT", writes=[("bqT",)])
        dma("sp", s_bkT[:], d_bkT, "bkT", writes=[("bkT",)])
        dma("sp", s_b1T[:], d_b1T, "b1T", writes=[("b1T",)])
        S.add("gp", lambda e: e.dma_start(s_bv[:], d_bvr), writes=[("bv",)], dma="bv")
        S.add("gp", lambda e: e.dma_start(s_bo[:], d_bor), writes=[("bo",)], dma="bo")
        S.add("gp", lambda e: e.dma_start(s_b2[:], d_b2r), writes=[("b2",)], dma="b2")
        S.add("dv", lambda e: e.memset(s_ones_r[:], 1.0), writes=[("ones",)])
        S.add("dv", lambda e: e.memset(s_ones_c[:], 1.0), writes=[("ones1",)])
        S.add("dv", lambda e: e.memset(s_ones_rf[:], 1.0), writes=[("ones2",)])
        S.add("dv", lambda e: e.memset(s_ones_cf[:], 1.0), writes=[("ones3",)])
        S.add("dv", lambda e: e.memset(s_eps[:], EPS), writes=[("eps",)])
        S.add("dv", lambda e: e.memset(s_gacc[:], 0.0), writes=[("gacc",)])

        ps = contextlib.ExitStack()
        s_cneg = ps.enter_context(nc.sbuf_tensor("s_cneg", [128, 80], F32))
        s_wqT = ps.enter_context(nc.sbuf_tensor("s_wqT", [128, DC * D], BF16))
        s_wkT = ps.enter_context(nc.sbuf_tensor("s_wkT", [128, DC * D], BF16))
        s_wvT = ps.enter_context(nc.sbuf_tensor("s_wvT", [128, DC * D], BF16))
        s_KT = ps.enter_context(nc.sbuf_tensor("s_KT", [128, 4 * T], BF16))
        s_V = ps.enter_context(nc.sbuf_tensor("s_V", [128, NT * 768], BF16))
        s_QT = [ps.enter_context(nc.sbuf_tensor(f"s_QT{i}", [128, 4 * 512], BF16))
                for i in range(2)]
        s_ln1T = ps.enter_context(nc.sbuf_tensor("s_ln1T", [128, DC * 512], BF16))
        s_PT = [ps.enter_context(nc.sbuf_tensor(f"s_PT{i}", [128, 1024], BF16))
                for i in range(4)]
        s_OTst = [ps.enter_context(nc.sbuf_tensor(f"s_OTst{i}", [128, 512], BF16))
                  for i in range(2)]
        s_rrep = ps.enter_context(nc.sbuf_tensor("s_rrep", [128, 512], F32))
        s_xb = [ps.enter_context(nc.sbuf_tensor(f"s_xb{i}", [128, 1024], F32))
                for i in range(2)]
        s_ln1 = [ps.enter_context(nc.sbuf_tensor(f"s_ln1{i}", [128, 1024], F32))
                 for i in range(4)]
        s_gbuf = [ps.enter_context(nc.sbuf_tensor(f"s_gbuf{i}", [128, 1024], F32))
                  for i in range(2)]
        s_gred = [ps.enter_context(nc.sbuf_tensor(f"s_gred{i}", [128, 1], F32))
                  for i in range(2)]
        s_sq = ps.enter_context(nc.sbuf_tensor("s_sq", [128, 1024], BF16))
        s_st12 = [ps.enter_context(nc.sbuf_tensor(f"s_st12_{i}", [128, 12], F32))
                  for i in range(2)]
        s_st2 = [ps.enter_context(nc.sbuf_tensor(f"s_st2_{i}", [128, 2], F32))
                 for i in range(2)]
        s_rstd = [ps.enter_context(nc.sbuf_tensor(f"s_rstd{i}", [128, 1], F32))
                  for i in range(2)]
        s_nb = ps.enter_context(nc.sbuf_tensor("s_nb", [1, 80], F32))
        s_bvrep = ps.enter_context(nc.sbuf_tensor("s_bvrep", [128, 512], F32))

        for nm, sb, dr in (("wq", s_wqT, d_wqT), ("wk", s_wkT, d_wkT),
                           ("wv", s_wvT, d_wvT)):
            dma("gp", sb[:].rearrange("p (c d) -> p c d", c=DC),
                dr.rearrange("(c p) d -> p c d", p=128), nm, writes=[(nm,)])

        for nm, sb, pp, ppk in (("wq", s_wqT, pBA, "pBA"),
                                ("wk", s_wkT, pBB, "pBB")):
            for c in range(DC):
                S.add("ac", lambda e, c=c, sb=sb: e.activation(
                    s_sq[:], sb[:, c * D:(c + 1) * D], AF.Square),
                    reads=[(nm,)], writes=[("sq",)])
                for half in range(2):
                    S.add("te", lambda e, c=c, half=half, pp=pp: e.matmul(
                        pp[0:1, half * 512: half * 512 + 512],
                        s_ones_c[:], s_sq[:, half * 512: half * 512 + 512],
                        start=(c == 0), stop=(c == DC - 1)),
                        reads=[("sq",), ("ones1",)], writes=[(ppk,)])
        S.add("dv", lambda e: e.reduce_max(
            s_nb[0:1, 0:16], pBA[0:1, 0:1024].rearrange("p (h d) -> p h d", h=16),
            AX.X), reads=[("pBA",)], writes=[("nb",)])
        S.add("dv", lambda e: e.reduce_max(
            s_nb[0:1, 16:32], pBB[0:1, 0:1024].rearrange("p (h d) -> p h d", h=16),
            AX.X), reads=[("pBB",)], writes=[("nb",)])
        with nc.sbuf_tensor("s_brow", [1, D], F32) as s_brow:
            for bi, dr in enumerate((d_bqr, d_bkr)):
                S.add("gp", lambda e, dr=dr: e.dma_start(s_brow[:], dr),
                      writes=[("brow",)], dma="brq")
                S.add("ac", lambda e: e.activation(
                    s_brow[:], s_brow[:], AF.Square),
                    reads=[("brow",)], writes=[("brow",)])
                S.add("dv", lambda e, bi=bi: e.reduce_sum(
                    s_nb[0:1, 32 + 16 * bi:48 + 16 * bi],
                    s_brow[:].rearrange("p (h d) -> p h d", d=64), AX.X),
                    reads=[("brow",)], writes=[("nb",), ("brow",)])
        S.add("ac", lambda e: e.activation(
            s_nb[0:1, 0:64], s_nb[0:1, 0:64], AF.Ln),
            reads=[("nb",)], writes=[("nb",)])
        S.add("ac", lambda e: e.activation(
            s_nb[0:1, 0:64], s_nb[0:1, 0:64], AF.Exp, scale=0.5),
            reads=[("nb",)], writes=[("nb",)])
        S.add("dv", lambda e: e.tensor_scalar(
            s_nb[0:1, 0:32], s_nb[0:1, 0:32], 32.2, None, ALU.mult),
            reads=[("nb",)], writes=[("nb",)])
        S.add("dv", lambda e: e.tensor_add(
            s_nb[0:1, 0:32], s_nb[0:1, 0:32], s_nb[0:1, 32:64]),
            reads=[("nb",)], writes=[("nb",)])
        S.add("dv", lambda e: e.tensor_mul(
            s_nb[0:1, 0:16], s_nb[0:1, 0:16], s_nb[0:1, 16:32]),
            reads=[("nb",)], writes=[("nb",)])
        S.add("dv", lambda e: e.tensor_scalar(
            s_nb[0:1, 0:16], s_nb[0:1, 0:16], -1.01 * 0.125, -1.0,
            ALU.mult, ALU.add), reads=[("nb",)], writes=[("nb",)])
        S.add("ac", lambda e: e.activation(
            s_nb[0:1, 16:32], s_nb[0:1, 0:16], AF.Exp, scale=-1.0),
            reads=[("nb",)], writes=[("nb",)])
        S.add("dv", lambda e: e.tensor_scalar(
            s_nb[0:1, 32:48], s_nb[0:1, 0:16], -1.0, None, ALU.mult),
            reads=[("nb",)], writes=[("nb",)])
        nb2 = s_nb[0:1, 0:16].rearrange("p (j t) -> p j t", t=2)
        S.add("dv", lambda e, nb2=nb2: e.tensor_tensor(
            s_nb[0:1, 48:56], nb2[:, :, 0:1], nb2[:, :, 1:2], ALU.min),
            reads=[("nb",)], writes=[("nb",)])
        S.add("ac", lambda e: e.activation(
            s_nb[0:1, 56:64], s_nb[0:1, 48:56], AF.Exp, scale=-1.0),
            reads=[("nb",)], writes=[("nb",)])
        S.add("dv", lambda e: e.tensor_scalar(
            s_nb[0:1, 64:72], s_nb[0:1, 48:56], -1.0, None, ALU.mult),
            reads=[("nb",)], writes=[("nb",)])
        S.add("te", lambda e: e.matmul(
            pSC[:, 0:72], s_ones_rf[:], s_nb[0:1, 0:72], start=True, stop=True),
            reads=[("nb",), ("ones2",)], writes=[("pSC",)])
        S.add("dv", lambda e: e.tensor_copy(s_cneg[:, 0:72], pSC[:, 0:72]),
              reads=[("pSC",)], writes=[("cneg",)])
        S.add("dv", lambda e: e.memset(
            s_V[:].rearrange("p (t pr a) -> p t pr a", pr=4, a=192)[:, :, :, 64:128],
            1.0), writes=[("vones",)])

        gamma_chunks = []
        for half in range(2 * D // 128 // 2):
            pass
        d_w1Tf = d_w1T.rearrange("p (c d) -> (p c) d", c=2)
        for i in range(32):
            gamma_chunks.append((d_w1Tf, i, None, 0))
        for rc in range(32):
            gamma_chunks.append((d_w2T, rc, None, 1))
        gamma_i = [0]
        gamma_per_tile = -(-len(gamma_chunks) // (4 * NG + 4))

        def emit_gamma_chunk():
            for _ in range(gamma_per_tile):
                if gamma_i[0] >= len(gamma_chunks):
                    return
                idx = gamma_i[0]
                gamma_i[0] += 1
                dr, rc, hh, col = gamma_chunks[idx]
                b = idx % 2
                src = dr[rc * 128:(rc + 1) * 128, :]
                dst = s_gbuf[b][:]
                S.add("sp", lambda e, dst=dst, src=src: e.dma_start(dst, src),
                      writes=[("gbuf", b)], dma="gb%d" % b)
                S.add("dv", lambda e, b=b: e.reduce_sum(
                    s_gred[b][:], s_gbuf[b][:], AX.X, apply_absolute_value=True),
                    reads=[("gbuf", b)], writes=[("gred", b)])
                S.add("dv", lambda e, b=b, col=col: e.tensor_add(
                    s_gacc[:, col:col + 1], s_gacc[:, col:col + 1],
                    s_gred[b][:]),
                    reads=[("gred", b)], writes=[("gacc",)])

        def prep_a(t):
            b = t % 2
            dma("sp", s_xb[b][:], d_x[t * 128:(t + 1) * 128, :], "xb%d" % b,
                writes=[("xb", b)])
            S.add("dv", lambda e, b=b: e.bn_stats(
                s_st12[b][:, 0:6], s_xb[b][:, 0:512]),
                reads=[("xb", b)], writes=[("st12a", b)])
            S.add("dv", lambda e, b=b: e.bn_stats(
                s_st12[b][:, 6:12], s_xb[b][:, 512:1024]),
                reads=[("xb", b)], writes=[("st12b", b)])
            S.add("dv", lambda e, b=b: e.bn_aggr(s_st2[b][:], s_st12[b][:]),
                  reads=[("st12a", b), ("st12b", b)], writes=[("st2", b)])
            S.add("ac", lambda e, b=b: e.activation(
                s_rstd[b][:], s_st2[b][:, 1:2], AF.Ln, bias=s_eps[:]),
                reads=[("st2", b), ("eps",)], writes=[("rstd", b)])
            S.add("ac", lambda e, b=b: e.activation(
                s_rstd[b][:], s_rstd[b][:], AF.Exp, scale=-0.5),
                reads=[("rstd", b)], writes=[("rstd", b)])
            S.add("dv", lambda e, b=b, b4=t % 4: e.tensor_scalar(
                s_ln1[b4][:], s_xb[b][:], s_st2[b][:, 0:1], s_rstd[b][:],
                ALU.subtract, ALU.mult),
                reads=[("xb", b), ("st2", b), ("rstd", b)], writes=[("ln1", t % 4)])
            emit_gamma_chunk()

        def prep_b(t):
            b = t % 4
            tl = t % 4
            for g2 in range(2):
                pp = pSC if g2 == 0 else pSD
                pk = "pSC" if g2 == 0 else "pSD"
                for c4 in range(4):
                    c = g2 * 4 + c4
                    S.add("te", lambda e, pp=pp, c4=c4, c=c, b=b: e.transpose(
                        pp[:, c4 * 128:(c4 + 1) * 128],
                        s_ln1[b][:, c * 128:(c + 1) * 128], s_ident[:]),
                        reads=[("ln1", b), ("ident",)], writes=[(pk,)])
                dst = s_ln1T[:].rearrange("p (c x) -> p c x", x=512)[
                    :, g2 * 4:(g2 + 1) * 4, tl * 128:(tl + 1) * 128]
                src = pp[:].rearrange("p (c x) -> p c x", x=128)
                S.add("dv", lambda e, src=src, dst=dst: e.tensor_copy(dst, src),
                      reads=[(pk,)], writes=[("ln1T", g2 * 4 + 0, tl),
                                             ("ln1T", g2 * 4 + 1, tl),
                                             ("ln1T", g2 * 4 + 2, tl),
                                             ("ln1T", g2 * 4 + 3, tl)])

        LN1T_KEYS = [("ln1T", c, tl) for c in range(DC) for tl in range(4)]

        def qkv_q(p, g, j, pp=None, ppk=None):
            gpair = 4 * p + j
            qb = g % 2
            ln1T_keys = LN1T_KEYS
            pp = pp if pp is not None else pSC
            ppk = ppk if ppk is not None else "pSC"
            for c in range(DC):
                S.add("te", lambda e, c=c, j=j, gpair=gpair, pp=pp: e.matmul(
                    pp[:, 0:512],
                    s_wqT[:, c * D + gpair * 128: c * D + gpair * 128 + 128],
                    s_ln1T[:, c * 512:(c + 1) * 512],
                    start=(c == 0), stop=(c == DC - 1)),
                    reads=ln1T_keys + [("wq",)], writes=[(ppk,)])
            S.add("dv", lambda e, j=j, qb=qb, gpair=gpair, pp=pp: e.tensor_scalar(
                s_QT[qb][:, j * 512:(j + 1) * 512], pp[:, 0:512],
                s_bqT[:, gpair:gpair + 1], None, ALU.add),
                reads=[(ppk,), ("bqT",)], writes=[("QT", qb, j)])

        def qkv_k(p, g, j, pp=None, ppk=None):
            gpair = 4 * p + j
            ln1T_keys = LN1T_KEYS
            pp = pp if pp is not None else pSD
            ppk = ppk if ppk is not None else "pSD"
            for c in range(DC):
                S.add("te", lambda e, c=c, j=j, gpair=gpair, pp=pp: e.matmul(
                    pp[:, 0:512],
                    s_wkT[:, c * D + gpair * 128: c * D + gpair * 128 + 128],
                    s_ln1T[:, c * 512:(c + 1) * 512],
                    start=(c == 0), stop=(c == DC - 1)),
                    reads=ln1T_keys + [("wk",)], writes=[(ppk,)])
            S.add("dv", lambda e, j=j, g=g, gpair=gpair, pp=pp: e.tensor_scalar(
                s_KT[:, j * T + g * 512: j * T + (g + 1) * 512], pp[:, 0:512],
                s_bkT[:, gpair:gpair + 1], None, ALU.add),
                reads=[(ppk,), ("bkT",)], writes=[("KT", j, g)])

        def qkv_v(p, g, j, pp=None, ppk=None):
            ln1T_keys = LN1T_KEYS
            t = 4 * g + j
            tl = j
            pp = pp if pp is not None else pSC
            ppk = ppk if ppk is not None else "pSC"
            for c in range(DC):
                S.add("te", lambda e, c=c, tl=tl, p=p, pp=pp: e.matmul(
                    pp[:, 0:512],
                    s_ln1T[:, c * 512 + tl * 128: c * 512 + tl * 128 + 128],
                    s_wvT[:, c * D + p * 512: c * D + (p + 1) * 512],
                    start=(c == 0), stop=(c == DC - 1)),
                    reads=ln1T_keys + [("wv",)], writes=[(ppk,)])
            v3 = s_V[:, t * 768:(t + 1) * 768].rearrange(
                "p (pr a) -> p pr a", pr=4)
            p4 = pp[:, 0:512].rearrange("p (pr s q) -> p pr s q", pr=4, s=2)
            b4 = s_bvrep[:].rearrange("p (pr s q) -> p pr s q", pr=4, s=2)
            S.add("dv", lambda e, v3=v3, p4=p4, b4=b4: e.tensor_add(
                v3[:, :, 0:64], p4[:, :, 0:1, :], b4[:, :, 0:1, :]),
                reads=[(ppk,), ("bvrep",)], writes=[("V", t)])
            S.add("dv", lambda e, v3=v3, p4=p4, b4=b4: e.tensor_add(
                v3[:, :, 128:192], p4[:, :, 1:2, :], b4[:, :, 1:2, :]),
                reads=[(ppk,), ("bvrep",)], writes=[("V", t)])

        def pair_bundle(p, g, pr, fill):
            qb = g % 2
            pg = 4 * p + pr
            nchunks = 4 * g + 4
            pend = []

            def emit_pv(c, bb):
                t0 = max(0, c - 4 * g)
                for par in range(2):
                    pOT = pSA if par == 0 else pSB
                    pOTk = "pSA" if par == 0 else "pSB"
                    voff = c * 768 + pr * 192 + par * 64
                    vsl = s_V[:, voff: voff + 128]
                    if t0 == 0:
                        S.add("te", lambda e, pOT=pOT, vsl=vsl, bb=bb,
                              par=par, c=c: e.matmul(
                            pOT[:, 0:512], vsl,
                            s_PT[bb][:, par * 512:(par + 1) * 512],
                            start=(c == 0), stop=(c == nchunks - 1)),
                            reads=[("PT", bb), ("V", c), ("vones",)],
                            writes=[(pOTk,)])
                    else:
                        for tl in range(t0, 4):
                            S.add("te", lambda e, pOT=pOT, vsl=vsl, bb=bb,
                                  par=par, c=c, tl=tl: e.matmul(
                                pOT[:, tl * 128:(tl + 1) * 128], vsl,
                                s_PT[bb][:, par * 512 + tl * 128:
                                          par * 512 + (tl + 1) * 128],
                                start=False,
                                stop=(c == nchunks - 1 and tl == 3)),
                                reads=[("PT", bb), ("V", c), ("vones",)],
                                writes=[(pOTk,)])

            for c in range(nchunks):
                bb = c % 4
                pST = pBA if c % 2 == 0 else pBB
                pSTk = "pBA" if c % 2 == 0 else "pBB"
                for par in range(2):
                    S.add("te", lambda e, pST=pST, c=c, pr=pr, par=par,
                          qb=qb: e.matmul(
                        pST[:, par * 512:(par + 1) * 512],
                        s_KT[par * 64: par * 64 + 64,
                             pr * T + c * 128: pr * T + (c + 1) * 128],
                        s_QT[qb][par * 64: par * 64 + 64,
                                 pr * 512:(pr + 1) * 512],
                        start=True, stop=True),
                        reads=[("KT", pr, c // 4), ("QT", qb, pr)],
                        writes=[(pSTk,)])
                lo = max(0, (c - 4 * g)) * 128
                if lo == 0:
                    S.add("ac", lambda e, pST=pST, bb=bb, pg=pg: e.activation(
                        s_PT[bb][:], pST[:], AF.Exp,
                        bias=s_cneg[:, 48 + pg:49 + pg], scale=0.125),
                        reads=[(pSTk,), ("cneg",)], writes=[("PT", bb)])
                else:
                    for par in range(2):
                        S.add("ac", lambda e, pST=pST, bb=bb, pg=pg, par=par,
                              lo=lo: e.activation(
                            s_PT[bb][:, par * 512 + lo:(par + 1) * 512],
                            pST[:, par * 512 + lo:(par + 1) * 512], AF.Exp,
                            bias=s_cneg[:, 48 + pg:49 + pg], scale=0.125),
                            reads=[(pSTk,), ("cneg",)], writes=[("PT", bb)])
                tl = c - 4 * g
                if 0 <= tl <= 3:
                    sl = bass.AP(s_PT[bb], tl * 128,
                                 [[1024, 128], [512, 2], [1, 128]])
                    t2 = bass.AP(s_tri, 0, [[128, 128], [0, 2], [1, 128]])
                    S.add("dv", lambda e, sl=sl, t2=t2: e.tensor_mul(
                        sl, sl, t2),
                        reads=[("PT", bb), ("tri",)], writes=[("PT", bb)])
                if len(pend) == 2:
                    emit_pv(*pend.pop(0))
                pend.append((c, bb))
                if fill and c % 2 == 1:
                    fill.pop(0)()
            for args in pend:
                emit_pv(*args)
            sb = pr % 2
            S.add("ac", lambda e, pg=pg: e.activation(
                s_rrep[0:64, :], pSA[64:128, 0:512], AF.Ln,
                scale=s_cneg[64:128, 56 + pg:57 + pg]),
                reads=[("pSA",), ("cneg",)], writes=[("rrep",)])
            S.add("ac", lambda e, pg=pg: e.activation(
                s_rrep[64:128, :], pSB[0:64, 0:512], AF.Ln,
                scale=s_cneg[0:64, 56 + pg:57 + pg]),
                reads=[("pSB",), ("cneg",)], writes=[("rrep",)])
            S.add("ac", lambda e, pg=pg: e.activation(
                s_rrep[:], s_rrep[:], AF.Exp, scale=-1.0,
                bias=s_cneg[:, 64 + pg:65 + pg]),
                reads=[("rrep",), ("cneg",)], writes=[("rrep",)])
            S.add("dv", lambda e, sb=sb: e.tensor_mul(
                s_OTst[sb][0:64, :], pSA[0:64, 0:512], s_rrep[0:64, :]),
                reads=[("pSA",), ("rrep",)], writes=[("OTst", sb, 0)])
            S.add("dv", lambda e, sb=sb: e.tensor_mul(
                s_OTst[sb][64:128, :], pSB[64:128, 0:512], s_rrep[64:128, :]),
                reads=[("pSB",), ("rrep",)], writes=[("OTst", sb, 1)])
            gpair = pg
            dma("gp", d_ots[:, 4 * g:4 * g + 4, gpair, :],
                s_OTst[sb][:].rearrange("p (t s) -> p t s", t=4),
                "spill%d" % sb,
                reads=[("OTst", sb, 0), ("OTst", sb, 1)],
                writes=[("OTS", gpair, g)])

        def bv_replicate(p):
            S.add("te", lambda e, p=p: e.matmul(
                pSD[:, 0:512], s_ones_r[:],
                s_bv[0:1, p * 512:(p + 1) * 512], start=True, stop=True),
                reads=[("bv",), ("ones",)], writes=[("pSD",)])
            S.add("dv", lambda e: e.tensor_copy(s_bvrep[:], pSD[:, 0:512]),
                  reads=[("pSD",)], writes=[("bvrep",)])

        bv_replicate(0)
        for t in range(4):
            prep_a(t)
        for t in range(4):
            prep_b(t)
        for j in range(4):
            qkv_q(0, 0, j, pSC, "pSC")
            qkv_k(0, 0, j, pSD, "pSD")
            qkv_v(0, 0, j, pSC if j % 2 else pSD, "pSC" if j % 2 else "pSD")

        seq = [(p, g) for p in range(2) for g in range(NG)]
        for si, (p, g) in enumerate(seq):
            nxt = seq[si + 1] if si + 1 < len(seq) else None
            fill = []
            tail = []
            if nxt is not None:
                np_, ng_ = nxt
                for tl in range(4):
                    fill.append(lambda tl=tl, ng_=ng_: prep_a(4 * ng_ + tl))
                for tl in range(4):
                    fill.append(lambda tl=tl, ng_=ng_: prep_b(4 * ng_ + tl))
                rot = [0]
                for j in range(4):
                    for fn in (qkv_q, qkv_k, qkv_v):
                        r = rot[0] % 2
                        rot[0] += 1
                        bank = (pSC, "pSC") if r == 0 else (pSD, "pSD")
                        qb_fn = (lambda np_=np_, ng_=ng_, j=j, fn=fn,
                                 bank=bank: fn(np_, ng_, j, *bank))
                        (tail if np_ != p else fill).append(qb_fn)
            for pr in range(4):
                pair_bundle(p, g, pr, fill)
            for b in fill:
                b()
            fill.clear()
            if tail:
                bv_replicate(1)
            for b in tail:
                b()
            if p == 0 and g == NG - 1:
                S.add("te", lambda e: e.matmul(
                    pSC[0:1, 0:2], s_ones_cf[:], s_gacc[:], start=True,
                    stop=True), reads=[("gacc",), ("ones3",)],
                    writes=[("pSC",)])
                S.add("dv", lambda e: e.tensor_scalar(
                    s_gs[0:1, 0:2], pSC[0:1, 0:2], 1.0 / (D * FF), EPS,
                    ALU.mult, ALU.add), reads=[("pSC",)], writes=[("gs",)])
                S.add("dv", lambda e: e.reciprocal(
                    s_gs[0:1, 0:2], s_gs[0:1, 0:2]),
                    reads=[("gs",)], writes=[("gs",)])
                S.add("te", lambda e: e.matmul(
                    pSD[:, 0:2], s_ones_rf[:], s_gs[0:1, 0:2], start=True,
                    stop=True), reads=[("gs",), ("ones2",)],
                    writes=[("pSD",)])
                S.add("dv", lambda e: e.tensor_copy(s_gs[:], pSD[:, 0:2]),
                      reads=[("pSD",)], writes=[("gs",)])

        if debug:
            dd = dbg_out("gs", [128, 2])
            dma("sp", dd, s_gs[:], "dbg_gs", reads=[("gs",)])
            dd2 = dbg_out("cneg", [128, 72])
            dma("sp", dd2, s_cneg[:, 0:72], "dbg_cneg", reads=[("cneg",)])

        ps.close()
        S.barrier("cd")

        cd = contextlib.ExitStack()
        s_woT = cd.enter_context(nc.sbuf_tensor("s_woT", [128, DC * D], BF16))
        s_w1qT = cd.enter_context(nc.sbuf_tensor("s_w1qT", [128, DC * FF], BF16))
        s_w2qT = cd.enter_context(nc.sbuf_tensor("s_w2qT", [128, FC * D], BF16))

        dma("gp", s_woT[:].rearrange("p (c d) -> p c d", c=DC),
            d_woT.rearrange("(c p) d -> p c d", p=128), "wo", writes=[("wo",)])
        s_otr = [cd.enter_context(nc.sbuf_tensor(f"s_otr{i}", [128, 1024],
                                                 BF16))
                 for i in range(4)]
        for t0 in range(2):
            S.add("sp", lambda e, t0=t0: e.dma_start(
                s_otr[t0][:].rearrange("p (a s) -> p a s", a=8),
                d_ots[:, t0, :, :]),
                reads=[("OTS", pair, 0) for pair in range(8)],
                writes=[("otr", t0)], dma="otr%d" % t0)

        cq = contextlib.ExitStack()
        s_wst = [cq.enter_context(nc.sbuf_tensor(f"s_wst{i}", [128, 2048], F32))
                 for i in range(4)]
        s_abs = cq.enter_context(nc.sbuf_tensor("s_abs", [128, 2048], F32))
        s_sg = cq.enter_context(nc.sbuf_tensor("s_sg", [128, 2048], BF16))
        s_p2 = cq.enter_context(nc.sbuf_tensor("s_p2", [128, 2048], BF16))
        s_m2 = s_sg

        def quant_w1(step, dst):
            b = step % 2
            src = d_w1T[step * 128:(step + 1) * 128, :]
            S.add("gp", lambda e, src=src, b=b: e.dma_start(s_wst[b][:], src),
                  writes=[("wst", b)], dma="wst%d" % b)
            S.add("ac", lambda e, b=b: e.activation(
                s_abs[:], s_wst[b][:], AF.Abs,
                scale=s_gs[:, 0:1]),
                reads=[("wst", b), ("gs",)], writes=[("abs",)])
            S.add("ac", lambda e, b=b: e.activation(
                s_sg[:], s_wst[b][:], AF.Sign),
                reads=[("wst", b)], writes=[("sg",)])
            S.add("dv", lambda e: e.tensor_scalar(
                s_abs[:], s_abs[:], 0.5, None, ALU.is_gt),
                reads=[("abs",)], writes=[("abs",)])
            S.add("dv", lambda e, dst=dst: e.tensor_mul(
                dst, s_abs[:], s_sg[:]),
                reads=[("abs",), ("sg",)], writes=[("w1q",)])

        def quant_w2(step, dst):
            b = 2 + step % 2
            for j in range(2):
                src = d_w2T[(2 * step + j) * 128:(2 * step + j + 1) * 128, :]
                S.add("sp", lambda e, src=src, b=b, j=j: e.dma_start(
                    s_wst[b][:, j * 1024:(j + 1) * 1024], src),
                    writes=[("wst", b)], dma="wst%d" % b)
            S.add("dv", lambda e, b=b: e.tensor_scalar(
                s_p2[:], s_wst[b][:], s_gs[:, 1:2], 0.5, ALU.mult, ALU.is_gt),
                reads=[("wst", b), ("gs",)], writes=[("p2",)])
            S.add("dv", lambda e, b=b: e.tensor_scalar(
                s_m2[:], s_wst[b][:], s_gs[:, 1:2], -0.5, ALU.mult, ALU.is_lt),
                reads=[("wst", b), ("gs",)], writes=[("sg",)])
            S.add("dv", lambda e, dst=dst: e.tensor_sub(
                dst, s_p2[:], s_m2[:]),
                reads=[("p2",), ("sg",)], writes=[("w2q",)])

        for step in range(16):
            rc, hh = divmod(step, 2)
            quant_w1(step, s_w1qT[:, rc * FF + hh * 2048:
                                  rc * FF + (hh + 1) * 2048])
            quant_w2(step, s_w2qT[:, 2 * step * D: 2 * (step + 1) * D])
        cq.close()
        S.barrier("cd2")

        s_hT = cd.enter_context(nc.sbuf_tensor("s_hT", [128, FC * 256], BF16))
        s_ln3T = cd.enter_context(nc.sbuf_tensor("s_ln3T", [128, DC * 512], BF16))
        s_x1 = [cd.enter_context(nc.sbuf_tensor(f"s_x1_{i}", [128, 1024],
                                                mybir.dt.float16))
                for i in range(4)]
        s_xb2 = [cd.enter_context(nc.sbuf_tensor(f"s_xb2_{i}", [128, 1024], F32))
                 for i in range(2)]
        s_outst = [cd.enter_context(nc.sbuf_tensor(f"s_outst{i}", [128, 1024], F32))
                   for i in range(1)]
        s_st12d = [cd.enter_context(nc.sbuf_tensor(f"s_st12d{i}", [128, 12], F32))
                   for i in range(4)]
        s_st2d = [cd.enter_context(nc.sbuf_tensor(f"s_st2d{i}", [128, 2], F32))
                  for i in range(4)]
        s_rstdd = [cd.enter_context(nc.sbuf_tensor(f"s_rstdd{i}", [128, 1], F32))
                   for i in range(4)]

        def ln_chain(b, srcs, out_ap, out_keys, out_srcs=None):
            rd = []
            for half, (ap, keys) in enumerate(srcs):
                S.add("dv", lambda e, b=b, ap=ap, half=half: e.bn_stats(
                    s_st12d[b][:, half * 6:half * 6 + 6], ap),
                    reads=list(keys), writes=[("dst12", b, half)])
                rd += list(keys)
            S.add("dv", lambda e, b=b: e.bn_aggr(s_st2d[b][:], s_st12d[b][:]),
                  reads=[("dst12", b, 0), ("dst12", b, 1)], writes=[("dst2", b)])
            S.add("ac", lambda e, b=b: e.activation(
                s_rstdd[b][:], s_st2d[b][:, 1:2], AF.Ln, bias=s_eps[:]),
                reads=[("dst2", b), ("eps",)], writes=[("drstd", b)])
            S.add("ac", lambda e, b=b: e.activation(
                s_rstdd[b][:], s_rstdd[b][:], AF.Exp, scale=-0.5),
                reads=[("drstd", b)], writes=[("drstd", b)])
            S.add("dv", lambda e, b=b: e.tensor_scalar(
                s_st2d[b][:, 0:1], s_st2d[b][:, 0:1], s_rstdd[b][:], -1.0,
                ALU.mult, ALU.mult),
                reads=[("dst2", b), ("drstd", b)], writes=[("dst2", b)])
            nsrc = out_srcs if out_srcs is not None else srcs
            for half, (ap, keys) in enumerate(nsrc):
                S.add("ac", lambda e, b=b, ap=ap, half=half, out_ap=out_ap:
                      e.activation(
                    out_ap[:, half * 512:(half + 1) * 512], ap, AF.Identity,
                    bias=s_st2d[b][:, 0:1], scale=s_rstdd[b][:]),
                    reads=list(keys) + [("dst2", b), ("drstd", b)],
                    writes=list(out_keys))

        w1keys = [("w1q",)]
        w2keys = [("w2q",)]
        NW = NT // 2

        def otr_load(t):
            if t >= NT:
                return
            S.add("sp", lambda e, t=t: e.dma_start(
                s_otr[t % 4][:].rearrange("p (a s) -> p a s", a=8),
                d_ots[:, t, :, :]),
                reads=[("OTS", pair, t // 4) for pair in range(8)],
                writes=[("otr", t % 4)], dma="otr%d" % (t % 4))

        def stage1a(t):
            tl = t % 2
            x4 = t % 4
            pA = pBA if tl == 0 else pBB
            pAk = "pBA" if tl == 0 else "pBB"
            otr_load(t + 2)
            dma("gp", s_xb2[tl][:], d_x[t * 128:(t + 1) * 128, :], "xb2%d" % tl,
                writes=[("xb2", tl)])
            for half in range(2):
                S.add("te", lambda e, pA=pA, half=half: e.matmul(
                    pA[:, half * 512:(half + 1) * 512], s_ones_r[:],
                    s_bo[0:1, half * 512:(half + 1) * 512],
                    start=True, stop=False),
                    reads=[("bo",), ("ones",)], writes=[(pAk,)])
                for c in range(DC):
                    S.add("te", lambda e, pA=pA, half=half, c=c, x4=x4:
                          e.matmul(
                        pA[:, half * 512:(half + 1) * 512],
                        s_otr[x4][:, c * 128:(c + 1) * 128],
                        s_woT[:, c * D + half * 512: c * D + (half + 1) * 512],
                        start=False, stop=(c == DC - 1)),
                        reads=[("otr", x4), ("wo",)], writes=[(pAk,)])
        def stage1ln(t):
            tl = t % 2
            x4 = t % 4
            pA = pBA if tl == 0 else pBB
            pAk = "pBA" if tl == 0 else "pBB"
            ln_chain(tl, [(pA[:, 0:512], [(pAk,)]), (pA[:, 512:1024], [(pAk,)])],
                     s_x1[x4][:], [("x1", x4)])
            S.add("dv", lambda e, x4=x4, tl=tl: e.tensor_add(
                s_x1[x4][:], s_x1[x4][:], s_xb2[tl][:]),
                reads=[("x1", x4), ("xb2", tl)], writes=[("x1", x4)])
            ln_chain(2 + tl,
                     [(s_x1[x4][:, 0:512], [("x1", x4)]),
                      (s_x1[x4][:, 512:1024], [("x1", x4)])],
                     s_xb2[tl][:], [("xb2", tl)])

        def stage1b(t):
            tl = t % 2
            wpar = (t // 2) % 2
            for g2 in range(2):
                pp = pSA if g2 == 0 else pSB
                pk = "pSA" if g2 == 0 else "pSB"
                for c4 in range(4):
                    c = g2 * 4 + c4
                    S.add("te", lambda e, pp=pp, c4=c4, c=c, tl=tl: e.transpose(
                        pp[:, c4 * 128:(c4 + 1) * 128],
                        s_xb2[tl][:, c * 128:(c + 1) * 128], s_ident[:]),
                        reads=[("xb2", tl), ("ident",)], writes=[(pk,)])
                dst = s_ln3T[:].rearrange("p (c x) -> p c x", x=512)[
                    :, g2 * 4:(g2 + 1) * 4,
                    wpar * 256 + tl * 128: wpar * 256 + (tl + 1) * 128]
                src3 = pp[:].rearrange("p (c x) -> p c x", x=128)
                S.add("ac", lambda e, src3=src3, dst=dst: e.activation(
                    dst, src3, AF.Identity),
                    reads=[(pk,)], writes=[("l3", wpar, tl, g2)])

        def stage2a(w):
            wpar = w % 2
            l3keys = [("l3", wpar, tl, g2) for tl in range(2) for g2 in range(2)]
            for fc in range(FC):
                ph = pSA if fc % 2 == 0 else pSB
                phk = "pSA" if fc % 2 == 0 else "pSB"
                for c in range(DC):
                    S.add("te", lambda e, ph=ph, fc=fc, c=c, wpar=wpar:
                          e.matmul(
                        ph[:, 256:512],
                        s_w1qT[:, c * FF + fc * 128: c * FF + (fc + 1) * 128],
                        s_ln3T[:, c * 512 + wpar * 256: c * 512 + wpar * 256 + 256],
                        start=(c == 0), stop=(c == DC - 1)),
                        reads=l3keys + w1keys, writes=[(phk,)])
                S.add("ac", lambda e, ph=ph, fc=fc: e.activation(
                    s_hT[:, fc * 256:(fc + 1) * 256], ph[:, 256:512], AF.Relu,
                    bias=s_b1T[:, fc:fc + 1]),
                    reads=[(phk,), ("b1T",)], writes=[("hT", fc)])

        hkeys = [("hT", fc) for fc in range(FC)]

        def stage2b(w, tl):
            t = 2 * w + tl
            x4 = t % 4
            for half in range(2):
                pO = pSC if half == 0 else pSD
                pOk = "pSC" if half == 0 else "pSD"
                S.add("te", lambda e, pO=pO, half=half: e.matmul(
                    pO[:, 0:512], s_ones_r[:],
                    s_b2[0:1, half * 512:(half + 1) * 512],
                    start=True, stop=False),
                    reads=[("b2",), ("ones",)], writes=[(pOk,)])
                for fc in range(FC):
                    S.add("te", lambda e, pO=pO, half=half, fc=fc, tl=tl:
                          e.matmul(
                        pO[:, 0:512],
                        s_hT[:, fc * 256 + tl * 128: fc * 256 + tl * 128 + 128],
                        s_w2qT[:, fc * D + half * 512:
                               fc * D + (half + 1) * 512],
                        start=False, stop=(fc == FC - 1)),
                        reads=hkeys + w2keys, writes=[(pOk,)])
            ln_chain(2 + tl,
                     [(pSC[:, 0:512], [("pSC",)]), (pSD[:, 0:512], [("pSD",)])],
                     s_outst[0][:], [("os", 0)])
            S.add("dv", lambda e, tl=tl, x4=x4: e.tensor_add(
                s_outst[0][:], s_outst[0][:], s_x1[x4][:]),
                reads=[("os", 0), ("x1", x4)], writes=[("os", 0)])
            dma("sp", d_y[t * 128:(t + 1) * 128, :], s_outst[0][:],
                "yst0", reads=[("os", 0)], writes=[("y", t)])

        emitted = set()

        def s1a_emit(t):
            if t < NT and t not in emitted:
                emitted.add(t)
                stage1a(t)

        prevw = None
        for w in range(NW):
            s1a_emit(2 * w)
            s1a_emit(2 * w + 1)
            if prevw is not None:
                stage2a(prevw)
            stage1ln(2 * w)
            stage1ln(2 * w + 1)
            if prevw is not None:
                stage2b(prevw, 0)
            stage1b(2 * w)
            if prevw is not None:
                s1a_emit(2 * w + 2)
                stage2b(prevw, 1)
            stage1b(2 * w + 1)
            prevw = w
        stage2a(prevw)
        stage2b(prevw, 0)
        stage2b(prevw, 1)

        S.add("sp", lambda e: e.nop(), reads=[("y", t) for t in range(NT)])

        cd.close()
        S.emit(nc, block, st)
    return nc, dbg


def _host_inputs(L0_attn_x, Wq, bq, Wk, bk, Wv, bv, Wo, bo, W1, b1, W2, b2,
                 T=4096):
    f32 = np.float32
    import ml_dtypes
    tri = np.triu(np.ones((128, 128), np.float32)).astype(ml_dtypes.bfloat16)
    base = {
        "wqT": np.ascontiguousarray(np.asarray(Wq, f32).T),
        "wkT": np.ascontiguousarray(np.asarray(Wk, f32).T),
        "wvT": np.ascontiguousarray(np.asarray(Wv, f32).T),
        "woT": np.ascontiguousarray(np.asarray(Wo, f32).T),
        "w1Tp": np.ascontiguousarray(
            np.asarray(W1, f32).T.reshape(8, 128, 2, 2048)
            .transpose(0, 2, 1, 3).reshape(2048, 2048)),
        "w2T": np.ascontiguousarray(np.asarray(W2, f32).T),
        "bqT": np.ascontiguousarray(np.asarray(bq, f32).reshape(8, 128).T),
        "bkT": np.ascontiguousarray(np.asarray(bk, f32).reshape(8, 128).T),
        "b1T": np.ascontiguousarray(np.asarray(b1, f32).reshape(32, 128).T),
        "bqr": np.asarray(bq, f32).reshape(1, D),
        "bkr": np.asarray(bk, f32).reshape(1, D),
        "bvr": np.asarray(bv, f32).reshape(1, D),
        "bor": np.asarray(bo, f32).reshape(1, D),
        "b2r": np.asarray(b2, f32).reshape(1, D),
        "ident": np.eye(128, dtype=f32),
        "tri": tri,
    }
    xs = np.asarray(L0_attn_x, f32)
    return [dict(base, x=np.ascontiguousarray(xs[i])) for i in range(xs.shape[0])]


def kernel(L0_attn_x, L0_attn_Wq, L0_attn_bq, L0_attn_Wk, L0_attn_bk,
           L0_attn_Wv, L0_attn_bv, L0_attn_Wo, L0_attn_bo,
           L0_ffn_W1, L0_ffn_b1, L0_ffn_W2, L0_ffn_b2):
    in_maps = _host_inputs(L0_attn_x, L0_attn_Wq, L0_attn_bq, L0_attn_Wk,
                           L0_attn_bk, L0_attn_Wv, L0_attn_bv, L0_attn_Wo,
                           L0_attn_bo, L0_ffn_W1, L0_ffn_b1, L0_ffn_W2,
                           L0_ffn_b2)
    nc, _ = build_program(T=4096)
    res = run_bass_kernel_spmd(nc, in_maps, core_ids=list(range(8)))
    out = np.stack([np.asarray(r["y"]) for r in res.results], axis=0)
    return out.astype(np.float32)


# version 47
# speedup vs baseline: 1.1948x; 1.1948x over previous
import contextlib
import numpy as np
import concourse.bass as bass
import concourse.mybir as mybir
from concourse.bass_utils import run_bass_kernel_spmd

F32 = mybir.dt.float32
BF16 = mybir.dt.bfloat16
AF = mybir.ActivationFunctionType
ALU = mybir.AluOpType
AX = mybir.AxisListType

D = 1024
NH = 16
DH = 64
FF = 4096
DC = 8
FC = 32
EPS = 1e-5

ENGINES = ("te", "dv", "ac", "sp", "gp")


class Sched:

    def __init__(self):
        self.ops = []
        self.res = {}
        self.last_dma = {}
        self.dma_cum = {}

    def add(self, eng, fn, reads=(), writes=(), dma=None, extra_deps=()):
        i = len(self.ops)
        deps = set(extra_deps)
        for k in reads:
            e = self.res.get(k)
            if e is not None and e["w"] is not None:
                deps.add(e["w"])
        for k in writes:
            e = self.res.get(k)
            if e is not None:
                if e["w"] is not None:
                    deps.add(e["w"])
                deps.update(e["r"])
        dmaidx = None
        if dma is not None:
            prev = self.last_dma.get(dma)
            if prev is not None:
                deps.add(prev)
            self.last_dma[dma] = i
            dmaidx = self.dma_cum.get(dma, 0) + 1
            self.dma_cum[dma] = dmaidx
        deps.discard(i)
        self.ops.append({
            "i": i, "eng": eng, "fn": fn, "deps": deps, "dma": dma,
            "dmaidx": dmaidx, "needed": False,
        })
        for k in writes:
            self.res[k] = {"w": i, "r": []}
        for k in reads:
            e = self.res.setdefault(k, {"w": None, "r": []})
            e["r"].append(i)
        return i

    def barrier(self, tag):
        last = {}
        for op in self.ops:
            if op["dma"] is None:
                last[op["eng"]] = op["i"]
        deps = tuple(last.values()) + tuple(self.last_dma.values())
        for e in ENGINES:
            self.add(e, lambda eng: eng.nop(), extra_deps=deps)
        self.res = {}

    def emit(self, nc, block, st):
        ops = self.ops
        for op in ops:
            best = {}
            for d in op["deps"]:
                p = ops[d]
                if p["eng"] == "te" and op["eng"] == "te" and p["dma"] is None \
                        and op["dma"] is None:
                    continue
                key = ("D", p["dma"]) if p["dma"] is not None else ("E", p["eng"])
                if key not in best or best[key] < d:
                    best[key] = d
            op["edeps"] = best
        for op in ops:
            for key, d in op["edeps"].items():
                if key[0] == "E":
                    ops[d]["needed"] = True
        cnt = {e: 0 for e in ENGINES}
        for op in ops:
            if op["dma"] is not None:
                continue
            if op["needed"]:
                cnt[op["eng"]] += 1
                op["inc_idx"] = cnt[op["eng"]]
        for op in ops:
            w = []
            for key, d in op["edeps"].items():
                p = ops[d]
                if key[0] == "D":
                    w.append(("D" + str(key[1]), 16 * p["dmaidx"]))
                else:
                    w.append(("E" + p["eng"], p["inc_idx"]))
            op["waits"] = w
        semnames = set("E" + e for e in ENGINES)
        semnames.update("D" + str(k) for k in self.dma_cum)
        sems = {n: st.enter_context(nc.semaphore(n[:60])) for n in sorted(semnames)}
        per_eng = {e: [op for op in ops if op["eng"] == e] for e in ENGINES}

        def mk(engops):
            def run(eng):
                clock = {}
                for op in engops:
                    for sname, val in sorted(op["waits"]):
                        if clock.get(sname, 0) < val:
                            eng.wait_ge(sems[sname], val)
                            clock[sname] = val
                    inst = op["fn"](eng)
                    if op["dma"] is not None:
                        inst.then_inc(sems["D" + str(op["dma"])], 16)
                    elif op["needed"]:
                        inst.then_inc(sems["E" + op["eng"]], 1)
            return run

        block.tensor(mk(per_eng["te"]))
        block.vector(mk(per_eng["dv"]))
        block.scalar(mk(per_eng["ac"]))
        block.sync(mk(per_eng["sp"]))
        block.gpsimd(mk(per_eng["gp"]))


def build_program(T=4096, debug=False):
    NT = T // 128
    NG = T // 512
    nc = bass.Bass("TRN2", target_bir_lowering=False)
    S = Sched()

    d_x = nc.dram_tensor("x", [T, D], F32, kind="ExternalInput").ap()
    d_wqT = nc.dram_tensor("wqT", [D, D], F32, kind="ExternalInput").ap()
    d_wkT = nc.dram_tensor("wkT", [D, D], F32, kind="ExternalInput").ap()
    d_wvT = nc.dram_tensor("wvT", [D, D], F32, kind="ExternalInput").ap()
    d_woT = nc.dram_tensor("woT", [D, D], F32, kind="ExternalInput").ap()
    d_w1T = nc.dram_tensor("w1Tp", [16 * 128, 2048], F32,
                           kind="ExternalInput").ap()
    d_w2T = nc.dram_tensor("w2T", [FF, D], F32, kind="ExternalInput").ap()
    d_bqT = nc.dram_tensor("bqT", [128, 8], F32, kind="ExternalInput").ap()
    d_bkT = nc.dram_tensor("bkT", [128, 8], F32, kind="ExternalInput").ap()
    d_b1T = nc.dram_tensor("b1T", [128, 32], F32, kind="ExternalInput").ap()
    d_bqr = nc.dram_tensor("bqr", [1, D], F32, kind="ExternalInput").ap()
    d_bkr = nc.dram_tensor("bkr", [1, D], F32, kind="ExternalInput").ap()
    d_bvr = nc.dram_tensor("bvr", [1, D], F32, kind="ExternalInput").ap()
    d_bor = nc.dram_tensor("bor", [1, D], F32, kind="ExternalInput").ap()
    d_b2r = nc.dram_tensor("b2r", [1, D], F32, kind="ExternalInput").ap()
    d_ident = nc.dram_tensor("ident", [128, 128], F32, kind="ExternalInput").ap()
    d_tri = nc.dram_tensor("tri", [128, 128], BF16, kind="ExternalInput").ap()
    d_y = nc.dram_tensor("y", [T, D], F32, kind="ExternalOutput").ap()
    d_ots = nc.dram_tensor("OTS", [128, T // 128, 8, 128], BF16).ap()
    dbg = {}

    def dbg_out(name, shape, dtype=F32):
        dbg[name] = nc.dram_tensor("dbg_" + name, shape, dtype,
                                   kind="ExternalOutput").ap()
        return dbg[name]

    st = contextlib.ExitStack()
    with st:
        block = st.enter_context(nc.Block())

        pBA = st.enter_context(nc.psum_tensor("pBA", [128, 1024], F32))
        pBB = st.enter_context(nc.psum_tensor("pBB", [128, 1024], F32))
        pSA = st.enter_context(nc.psum_tensor("pSA", [128, 512], F32))
        pSB = st.enter_context(nc.psum_tensor("pSB", [128, 512], F32))
        pSC = st.enter_context(nc.psum_tensor("pSC", [128, 512], F32))
        pSD = st.enter_context(nc.psum_tensor("pSD", [128, 512], F32))

        s_ident = st.enter_context(nc.sbuf_tensor("s_ident", [128, 128], F32))
        s_tri = st.enter_context(nc.sbuf_tensor("s_tri", [128, 128], BF16))
        s_bqT = st.enter_context(nc.sbuf_tensor("s_bqT", [128, 8], F32))
        s_bkT = st.enter_context(nc.sbuf_tensor("s_bkT", [128, 8], F32))
        s_b1T = st.enter_context(nc.sbuf_tensor("s_b1T", [128, 32], F32))
        s_bv = st.enter_context(nc.sbuf_tensor("s_bv", [1, D], BF16))
        s_bo = st.enter_context(nc.sbuf_tensor("s_bo", [1, D], BF16))
        s_b2 = st.enter_context(nc.sbuf_tensor("s_b2", [1, D], BF16))
        s_ones_r = st.enter_context(nc.sbuf_tensor("s_ones_r", [1, 128], BF16))
        s_ones_c = st.enter_context(nc.sbuf_tensor("s_ones_c", [128, 1], BF16))
        s_ones_rf = st.enter_context(nc.sbuf_tensor("s_ones_rf", [1, 128], F32))
        s_ones_cf = st.enter_context(nc.sbuf_tensor("s_ones_cf", [128, 1], F32))
        s_eps = st.enter_context(nc.sbuf_tensor("s_eps", [128, 1], F32))
        s_gs = st.enter_context(nc.sbuf_tensor("s_gs", [128, 2], F32))
        s_gacc = st.enter_context(nc.sbuf_tensor("s_gacc", [128, 2], F32))

        def dma(eng, out_ap, in_ap, key, reads=(), writes=()):
            S.add(eng, lambda e, o=out_ap, i=in_ap: e.dma_start(o, i),
                  reads=reads, writes=writes, dma=key)

        dma("sp", s_ident[:], d_ident, "ident", writes=[("ident",)])
        dma("sp", s_tri[:], d_tri, "tri", writes=[("tri",)])
        dma("sp", s_bqT[:], d_bqT, "bqT", writes=[("bqT",)])
        dma("sp", s_bkT[:], d_bkT, "bkT", writes=[("bkT",)])
        dma("sp", s_b1T[:], d_b1T, "b1T", writes=[("b1T",)])
        S.add("gp", lambda e: e.dma_start(s_bv[:], d_bvr), writes=[("bv",)], dma="bv")
        S.add("gp", lambda e: e.dma_start(s_bo[:], d_bor), writes=[("bo",)], dma="bo")
        S.add("gp", lambda e: e.dma_start(s_b2[:], d_b2r), writes=[("b2",)], dma="b2")
        S.add("dv", lambda e: e.memset(s_ones_r[:], 1.0), writes=[("ones",)])
        S.add("dv", lambda e: e.memset(s_ones_c[:], 1.0), writes=[("ones1",)])
        S.add("dv", lambda e: e.memset(s_ones_rf[:], 1.0), writes=[("ones2",)])
        S.add("dv", lambda e: e.memset(s_ones_cf[:], 1.0), writes=[("ones3",)])
        S.add("dv", lambda e: e.memset(s_eps[:], EPS), writes=[("eps",)])
        S.add("dv", lambda e: e.memset(s_gacc[:], 0.0), writes=[("gacc",)])

        ps = contextlib.ExitStack()
        s_cneg = ps.enter_context(nc.sbuf_tensor("s_cneg", [128, 80], F32))
        s_wqT = ps.enter_context(nc.sbuf_tensor("s_wqT", [128, DC * D], BF16))
        s_wkT = ps.enter_context(nc.sbuf_tensor("s_wkT", [128, DC * D], BF16))
        s_wvT = ps.enter_context(nc.sbuf_tensor("s_wvT", [128, DC * D], BF16))
        s_KT = ps.enter_context(nc.sbuf_tensor("s_KT", [128, 4 * T], BF16))
        s_V = ps.enter_context(nc.sbuf_tensor("s_V", [128, NT * 768], BF16))
        s_QT = [ps.enter_context(nc.sbuf_tensor(f"s_QT{i}", [128, 4 * 512], BF16))
                for i in range(2)]
        s_ln1T = ps.enter_context(nc.sbuf_tensor("s_ln1T", [128, DC * 512], BF16))
        s_PT = [ps.enter_context(nc.sbuf_tensor(f"s_PT{i}", [128, 1024], BF16))
                for i in range(4)]
        s_OTst = [ps.enter_context(nc.sbuf_tensor(f"s_OTst{i}", [128, 512], BF16))
                  for i in range(2)]
        s_rrep = ps.enter_context(nc.sbuf_tensor("s_rrep", [128, 512], F32))
        s_xb = [ps.enter_context(nc.sbuf_tensor(f"s_xb{i}", [128, 1024], F32))
                for i in range(2)]
        s_ln1 = [ps.enter_context(nc.sbuf_tensor(f"s_ln1{i}", [128, 1024], F32))
                 for i in range(4)]
        s_gbuf = [ps.enter_context(nc.sbuf_tensor(f"s_gbuf{i}", [128, 1024], F32))
                  for i in range(2)]
        s_gred = [ps.enter_context(nc.sbuf_tensor(f"s_gred{i}", [128, 1], F32))
                  for i in range(2)]
        s_sq = ps.enter_context(nc.sbuf_tensor("s_sq", [128, 1024], BF16))
        s_st12 = [ps.enter_context(nc.sbuf_tensor(f"s_st12_{i}", [128, 12], F32))
                  for i in range(2)]
        s_st2 = [ps.enter_context(nc.sbuf_tensor(f"s_st2_{i}", [128, 2], F32))
                 for i in range(2)]
        s_rstd = [ps.enter_context(nc.sbuf_tensor(f"s_rstd{i}", [128, 1], F32))
                  for i in range(2)]
        s_nb = ps.enter_context(nc.sbuf_tensor("s_nb", [1, 80], F32))
        s_bvrep = ps.enter_context(nc.sbuf_tensor("s_bvrep", [128, 512], F32))

        for nm, sb, dr in (("wq", s_wqT, d_wqT), ("wk", s_wkT, d_wkT),
                           ("wv", s_wvT, d_wvT)):
            dma("gp", sb[:].rearrange("p (c d) -> p c d", c=DC),
                dr.rearrange("(c p) d -> p c d", p=128), nm, writes=[(nm,)])

        for nm, sb, pp, ppk in (("wq", s_wqT, pBA, "pBA"),
                                ("wk", s_wkT, pBB, "pBB")):
            for c in range(DC):
                S.add("ac", lambda e, c=c, sb=sb: e.activation(
                    s_sq[:], sb[:, c * D:(c + 1) * D], AF.Square),
                    reads=[(nm,)], writes=[("sq",)])
                for half in range(2):
                    S.add("te", lambda e, c=c, half=half, pp=pp: e.matmul(
                        pp[0:1, half * 512: half * 512 + 512],
                        s_ones_c[:], s_sq[:, half * 512: half * 512 + 512],
                        start=(c == 0), stop=(c == DC - 1)),
                        reads=[("sq",), ("ones1",)], writes=[(ppk,)])
        S.add("dv", lambda e: e.reduce_max(
            s_nb[0:1, 0:16], pBA[0:1, 0:1024].rearrange("p (h d) -> p h d", h=16),
            AX.X), reads=[("pBA",)], writes=[("nb",)])
        S.add("dv", lambda e: e.reduce_max(
            s_nb[0:1, 16:32], pBB[0:1, 0:1024].rearrange("p (h d) -> p h d", h=16),
            AX.X), reads=[("pBB",)], writes=[("nb",)])
        with nc.sbuf_tensor("s_brow", [1, D], F32) as s_brow:
            for bi, dr in enumerate((d_bqr, d_bkr)):
                S.add("gp", lambda e, dr=dr: e.dma_start(s_brow[:], dr),
                      writes=[("brow",)], dma="brq")
                S.add("ac", lambda e: e.activation(
                    s_brow[:], s_brow[:], AF.Square),
                    reads=[("brow",)], writes=[("brow",)])
                S.add("dv", lambda e, bi=bi: e.reduce_sum(
                    s_nb[0:1, 32 + 16 * bi:48 + 16 * bi],
                    s_brow[:].rearrange("p (h d) -> p h d", d=64), AX.X),
                    reads=[("brow",)], writes=[("nb",), ("brow",)])
        S.add("ac", lambda e: e.activation(
            s_nb[0:1, 0:64], s_nb[0:1, 0:64], AF.Ln),
            reads=[("nb",)], writes=[("nb",)])
        S.add("ac", lambda e: e.activation(
            s_nb[0:1, 0:64], s_nb[0:1, 0:64], AF.Exp, scale=0.5),
            reads=[("nb",)], writes=[("nb",)])
        S.add("dv", lambda e: e.tensor_scalar(
            s_nb[0:1, 0:32], s_nb[0:1, 0:32], 32.2, None, ALU.mult),
            reads=[("nb",)], writes=[("nb",)])
        S.add("dv", lambda e: e.tensor_add(
            s_nb[0:1, 0:32], s_nb[0:1, 0:32], s_nb[0:1, 32:64]),
            reads=[("nb",)], writes=[("nb",)])
        S.add("dv", lambda e: e.tensor_mul(
            s_nb[0:1, 0:16], s_nb[0:1, 0:16], s_nb[0:1, 16:32]),
            reads=[("nb",)], writes=[("nb",)])
        S.add("dv", lambda e: e.tensor_scalar(
            s_nb[0:1, 0:16], s_nb[0:1, 0:16], -1.01 * 0.125, -1.0,
            ALU.mult, ALU.add), reads=[("nb",)], writes=[("nb",)])
        S.add("ac", lambda e: e.activation(
            s_nb[0:1, 16:32], s_nb[0:1, 0:16], AF.Exp, scale=-1.0),
            reads=[("nb",)], writes=[("nb",)])
        S.add("dv", lambda e: e.tensor_scalar(
            s_nb[0:1, 32:48], s_nb[0:1, 0:16], -1.0, None, ALU.mult),
            reads=[("nb",)], writes=[("nb",)])
        nb2 = s_nb[0:1, 0:16].rearrange("p (j t) -> p j t", t=2)
        S.add("dv", lambda e, nb2=nb2: e.tensor_tensor(
            s_nb[0:1, 48:56], nb2[:, :, 0:1], nb2[:, :, 1:2], ALU.min),
            reads=[("nb",)], writes=[("nb",)])
        S.add("ac", lambda e: e.activation(
            s_nb[0:1, 56:64], s_nb[0:1, 48:56], AF.Exp, scale=-1.0),
            reads=[("nb",)], writes=[("nb",)])
        S.add("dv", lambda e: e.tensor_scalar(
            s_nb[0:1, 64:72], s_nb[0:1, 48:56], -1.0, None, ALU.mult),
            reads=[("nb",)], writes=[("nb",)])
        S.add("te", lambda e: e.matmul(
            pSC[:, 0:72], s_ones_rf[:], s_nb[0:1, 0:72], start=True, stop=True),
            reads=[("nb",), ("ones2",)], writes=[("pSC",)])
        S.add("dv", lambda e: e.tensor_copy(s_cneg[:, 0:72], pSC[:, 0:72]),
              reads=[("pSC",)], writes=[("cneg",)])
        S.add("dv", lambda e: e.memset(
            s_V[:].rearrange("p (t pr a) -> p t pr a", pr=4, a=192)[:, :, :, 64:128],
            1.0), writes=[("vones",)])

        gamma_chunks = []
        for half in range(2 * D // 128 // 2):
            pass
        d_w1Tf = d_w1T.rearrange("p (c d) -> (p c) d", c=2)
        for i in range(32):
            gamma_chunks.append((d_w1Tf, i, None, 0))
        for rc in range(32):
            gamma_chunks.append((d_w2T, rc, None, 1))
        gamma_i = [0]
        gamma_per_tile = -(-len(gamma_chunks) // (4 * NG + 4))

        def emit_gamma_chunk():
            for _ in range(gamma_per_tile):
                if gamma_i[0] >= len(gamma_chunks):
                    return
                idx = gamma_i[0]
                gamma_i[0] += 1
                dr, rc, hh, col = gamma_chunks[idx]
                b = idx % 2
                src = dr[rc * 128:(rc + 1) * 128, :]
                dst = s_gbuf[b][:]
                S.add("sp", lambda e, dst=dst, src=src: e.dma_start(dst, src),
                      writes=[("gbuf", b)], dma="gb%d" % b)
                S.add("dv", lambda e, b=b: e.reduce_sum(
                    s_gred[b][:], s_gbuf[b][:], AX.X, apply_absolute_value=True),
                    reads=[("gbuf", b)], writes=[("gred", b)])
                S.add("dv", lambda e, b=b, col=col: e.tensor_add(
                    s_gacc[:, col:col + 1], s_gacc[:, col:col + 1],
                    s_gred[b][:]),
                    reads=[("gred", b)], writes=[("gacc",)])

        def prep_a(t):
            b = t % 2
            dma("sp", s_xb[b][:], d_x[t * 128:(t + 1) * 128, :], "xb%d" % b,
                writes=[("xb", b)])
            S.add("dv", lambda e, b=b: e.bn_stats(
                s_st12[b][:, 0:6], s_xb[b][:, 0:512]),
                reads=[("xb", b)], writes=[("st12a", b)])
            S.add("dv", lambda e, b=b: e.bn_stats(
                s_st12[b][:, 6:12], s_xb[b][:, 512:1024]),
                reads=[("xb", b)], writes=[("st12b", b)])
            S.add("dv", lambda e, b=b: e.bn_aggr(s_st2[b][:], s_st12[b][:]),
                  reads=[("st12a", b), ("st12b", b)], writes=[("st2", b)])
            S.add("ac", lambda e, b=b: e.activation(
                s_rstd[b][:], s_st2[b][:, 1:2], AF.Ln, bias=s_eps[:]),
                reads=[("st2", b), ("eps",)], writes=[("rstd", b)])
            S.add("ac", lambda e, b=b: e.activation(
                s_rstd[b][:], s_rstd[b][:], AF.Exp, scale=-0.5),
                reads=[("rstd", b)], writes=[("rstd", b)])
            S.add("dv", lambda e, b=b, b4=t % 4: e.tensor_scalar(
                s_ln1[b4][:], s_xb[b][:], s_st2[b][:, 0:1], s_rstd[b][:],
                ALU.subtract, ALU.mult),
                reads=[("xb", b), ("st2", b), ("rstd", b)], writes=[("ln1", t % 4)])
            emit_gamma_chunk()

        def prep_b(t):
            b = t % 4
            tl = t % 4
            for g2 in range(2):
                pp = pSC if g2 == 0 else pSD
                pk = "pSC" if g2 == 0 else "pSD"
                for c4 in range(4):
                    c = g2 * 4 + c4
                    S.add("te", lambda e, pp=pp, c4=c4, c=c, b=b: e.transpose(
                        pp[:, c4 * 128:(c4 + 1) * 128],
                        s_ln1[b][:, c * 128:(c + 1) * 128], s_ident[:]),
                        reads=[("ln1", b), ("ident",)], writes=[(pk,)])
                dst = s_ln1T[:].rearrange("p (c x) -> p c x", x=512)[
                    :, g2 * 4:(g2 + 1) * 4, tl * 128:(tl + 1) * 128]
                src = pp[:].rearrange("p (c x) -> p c x", x=128)
                S.add("dv", lambda e, src=src, dst=dst: e.tensor_copy(dst, src),
                      reads=[(pk,)], writes=[("ln1T", g2 * 4 + 0, tl),
                                             ("ln1T", g2 * 4 + 1, tl),
                                             ("ln1T", g2 * 4 + 2, tl),
                                             ("ln1T", g2 * 4 + 3, tl)])

        LN1T_KEYS = [("ln1T", c, tl) for c in range(DC) for tl in range(4)]

        def qkv_q(p, g, j, pp=None, ppk=None):
            gpair = 4 * p + j
            qb = g % 2
            ln1T_keys = LN1T_KEYS
            pp = pp if pp is not None else pSC
            ppk = ppk if ppk is not None else "pSC"
            for c in range(DC):
                S.add("te", lambda e, c=c, j=j, gpair=gpair, pp=pp: e.matmul(
                    pp[:, 0:512],
                    s_wqT[:, c * D + gpair * 128: c * D + gpair * 128 + 128],
                    s_ln1T[:, c * 512:(c + 1) * 512],
                    start=(c == 0), stop=(c == DC - 1)),
                    reads=ln1T_keys + [("wq",)], writes=[(ppk,)])
            S.add("dv", lambda e, j=j, qb=qb, gpair=gpair, pp=pp: e.tensor_scalar(
                s_QT[qb][:, j * 512:(j + 1) * 512], pp[:, 0:512],
                s_bqT[:, gpair:gpair + 1], None, ALU.add),
                reads=[(ppk,), ("bqT",)], writes=[("QT", qb, j)])

        def qkv_k(p, g, j, pp=None, ppk=None):
            gpair = 4 * p + j
            ln1T_keys = LN1T_KEYS
            pp = pp if pp is not None else pSD
            ppk = ppk if ppk is not None else "pSD"
            for c in range(DC):
                S.add("te", lambda e, c=c, j=j, gpair=gpair, pp=pp: e.matmul(
                    pp[:, 0:512],
                    s_wkT[:, c * D + gpair * 128: c * D + gpair * 128 + 128],
                    s_ln1T[:, c * 512:(c + 1) * 512],
                    start=(c == 0), stop=(c == DC - 1)),
                    reads=ln1T_keys + [("wk",)], writes=[(ppk,)])
            S.add("dv", lambda e, j=j, g=g, gpair=gpair, pp=pp: e.tensor_scalar(
                s_KT[:, j * T + g * 512: j * T + (g + 1) * 512], pp[:, 0:512],
                s_bkT[:, gpair:gpair + 1], None, ALU.add),
                reads=[(ppk,), ("bkT",)], writes=[("KT", j, g)])

        def qkv_v(p, g, j, pp=None, ppk=None):
            ln1T_keys = LN1T_KEYS
            t = 4 * g + j
            tl = j
            pp = pp if pp is not None else pSC
            ppk = ppk if ppk is not None else "pSC"
            for c in range(DC):
                S.add("te", lambda e, c=c, tl=tl, p=p, pp=pp: e.matmul(
                    pp[:, 0:512],
                    s_ln1T[:, c * 512 + tl * 128: c * 512 + tl * 128 + 128],
                    s_wvT[:, c * D + p * 512: c * D + (p + 1) * 512],
                    start=(c == 0), stop=(c == DC - 1)),
                    reads=ln1T_keys + [("wv",)], writes=[(ppk,)])
            v3 = s_V[:, t * 768:(t + 1) * 768].rearrange(
                "p (pr a) -> p pr a", pr=4)
            p4 = pp[:, 0:512].rearrange("p (pr s q) -> p pr s q", pr=4, s=2)
            b4 = s_bvrep[:].rearrange("p (pr s q) -> p pr s q", pr=4, s=2)
            S.add("dv", lambda e, v3=v3, p4=p4, b4=b4: e.tensor_add(
                v3[:, :, 0:64], p4[:, :, 0:1, :], b4[:, :, 0:1, :]),
                reads=[(ppk,), ("bvrep",)], writes=[("V", t)])
            S.add("dv", lambda e, v3=v3, p4=p4, b4=b4: e.tensor_add(
                v3[:, :, 128:192], p4[:, :, 1:2, :], b4[:, :, 1:2, :]),
                reads=[(ppk,), ("bvrep",)], writes=[("V", t)])

        def pair_bundle(p, g, pr, fill):
            qb = g % 2
            pg = 4 * p + pr
            nchunks = 4 * g + 4
            pend = []

            def emit_pv(c, bb):
                t0 = max(0, c - 4 * g)
                for par in range(2):
                    pOT = pSA if par == 0 else pSB
                    pOTk = "pSA" if par == 0 else "pSB"
                    voff = c * 768 + pr * 192 + par * 64
                    vsl = s_V[:, voff: voff + 128]
                    if t0 == 0:
                        S.add("te", lambda e, pOT=pOT, vsl=vsl, bb=bb,
                              par=par, c=c: e.matmul(
                            pOT[:, 0:512], vsl,
                            s_PT[bb][:, par * 512:(par + 1) * 512],
                            start=(c == 0), stop=(c == nchunks - 1)),
                            reads=[("PT", bb), ("V", c), ("vones",)],
                            writes=[(pOTk,)])
                    else:
                        for tl in range(t0, 4):
                            S.add("te", lambda e, pOT=pOT, vsl=vsl, bb=bb,
                                  par=par, c=c, tl=tl: e.matmul(
                                pOT[:, tl * 128:(tl + 1) * 128], vsl,
                                s_PT[bb][:, par * 512 + tl * 128:
                                          par * 512 + (tl + 1) * 128],
                                start=False,
                                stop=(c == nchunks - 1 and tl == 3)),
                                reads=[("PT", bb), ("V", c), ("vones",)],
                                writes=[(pOTk,)])

            for c in range(nchunks):
                bb = c % 4
                pST = pBA if c % 2 == 0 else pBB
                pSTk = "pBA" if c % 2 == 0 else "pBB"
                for par in range(2):
                    S.add("te", lambda e, pST=pST, c=c, pr=pr, par=par,
                          qb=qb: e.matmul(
                        pST[:, par * 512:(par + 1) * 512],
                        s_KT[par * 64: par * 64 + 64,
                             pr * T + c * 128: pr * T + (c + 1) * 128],
                        s_QT[qb][par * 64: par * 64 + 64,
                                 pr * 512:(pr + 1) * 512],
                        start=True, stop=True),
                        reads=[("KT", pr, c // 4), ("QT", qb, pr)],
                        writes=[(pSTk,)])
                lo = max(0, (c - 4 * g)) * 128
                if lo == 0:
                    S.add("ac", lambda e, pST=pST, bb=bb, pg=pg: e.activation(
                        s_PT[bb][:], pST[:], AF.Exp,
                        bias=s_cneg[:, 48 + pg:49 + pg], scale=0.125),
                        reads=[(pSTk,), ("cneg",)], writes=[("PT", bb)])
                else:
                    for par in range(2):
                        S.add("ac", lambda e, pST=pST, bb=bb, pg=pg, par=par,
                              lo=lo: e.activation(
                            s_PT[bb][:, par * 512 + lo:(par + 1) * 512],
                            pST[:, par * 512 + lo:(par + 1) * 512], AF.Exp,
                            bias=s_cneg[:, 48 + pg:49 + pg], scale=0.125),
                            reads=[(pSTk,), ("cneg",)], writes=[("PT", bb)])
                tl = c - 4 * g
                if 0 <= tl <= 3:
                    sl = bass.AP(s_PT[bb], tl * 128,
                                 [[1024, 128], [512, 2], [1, 128]])
                    t2 = bass.AP(s_tri, 0, [[128, 128], [0, 2], [1, 128]])
                    S.add("dv", lambda e, sl=sl, t2=t2: e.tensor_mul(
                        sl, sl, t2),
                        reads=[("PT", bb), ("tri",)], writes=[("PT", bb)])
                if len(pend) == 2:
                    emit_pv(*pend.pop(0))
                pend.append((c, bb))
                if fill and c % 2 == 1:
                    fill.pop(0)()
            for args in pend:
                emit_pv(*args)
            sb = pr % 2
            S.add("ac", lambda e, pg=pg: e.activation(
                s_rrep[0:64, :], pSA[64:128, 0:512], AF.Ln,
                scale=s_cneg[64:128, 56 + pg:57 + pg]),
                reads=[("pSA",), ("cneg",)], writes=[("rrep",)])
            S.add("ac", lambda e, pg=pg: e.activation(
                s_rrep[64:128, :], pSB[0:64, 0:512], AF.Ln,
                scale=s_cneg[0:64, 56 + pg:57 + pg]),
                reads=[("pSB",), ("cneg",)], writes=[("rrep",)])
            S.add("ac", lambda e, pg=pg: e.activation(
                s_rrep[:], s_rrep[:], AF.Exp, scale=-1.0,
                bias=s_cneg[:, 64 + pg:65 + pg]),
                reads=[("rrep",), ("cneg",)], writes=[("rrep",)])
            S.add("dv", lambda e, sb=sb: e.tensor_mul(
                s_OTst[sb][0:64, :], pSA[0:64, 0:512], s_rrep[0:64, :]),
                reads=[("pSA",), ("rrep",)], writes=[("OTst", sb, 0)])
            S.add("dv", lambda e, sb=sb: e.tensor_mul(
                s_OTst[sb][64:128, :], pSB[64:128, 0:512], s_rrep[64:128, :]),
                reads=[("pSB",), ("rrep",)], writes=[("OTst", sb, 1)])
            gpair = pg
            dma("gp", d_ots[:, 4 * g:4 * g + 4, gpair, :],
                s_OTst[sb][:].rearrange("p (t s) -> p t s", t=4),
                "spill%d" % sb,
                reads=[("OTst", sb, 0), ("OTst", sb, 1)],
                writes=[("OTS", gpair, g)])

        def bv_replicate(p):
            S.add("te", lambda e, p=p: e.matmul(
                pSD[:, 0:512], s_ones_r[:],
                s_bv[0:1, p * 512:(p + 1) * 512], start=True, stop=True),
                reads=[("bv",), ("ones",)], writes=[("pSD",)])
            S.add("dv", lambda e: e.tensor_copy(s_bvrep[:], pSD[:, 0:512]),
                  reads=[("pSD",)], writes=[("bvrep",)])

        bv_replicate(0)
        for t in range(4):
            prep_a(t)
        for t in range(4):
            prep_b(t)
        for j in range(4):
            qkv_q(0, 0, j, pSC, "pSC")
            qkv_k(0, 0, j, pSD, "pSD")
            qkv_v(0, 0, j, pSC if j % 2 else pSD, "pSC" if j % 2 else "pSD")

        seq = [(p, g) for p in range(2) for g in range(NG)]
        for si, (p, g) in enumerate(seq):
            nxt = seq[si + 1] if si + 1 < len(seq) else None
            fill = []
            tail = []
            if nxt is not None:
                np_, ng_ = nxt
                for tl in range(4):
                    fill.append(lambda tl=tl, ng_=ng_: prep_a(4 * ng_ + tl))
                for tl in range(4):
                    fill.append(lambda tl=tl, ng_=ng_: prep_b(4 * ng_ + tl))
                rot = [0]
                for j in range(4):
                    for fn in (qkv_q, qkv_k, qkv_v):
                        r = rot[0] % 2
                        rot[0] += 1
                        bank = (pSC, "pSC") if r == 0 else (pSD, "pSD")
                        qb_fn = (lambda np_=np_, ng_=ng_, j=j, fn=fn,
                                 bank=bank: fn(np_, ng_, j, *bank))
                        (tail if np_ != p else fill).append(qb_fn)
            for pr in range(4):
                pair_bundle(p, g, pr, fill)
            for b in fill:
                b()
            fill.clear()
            if tail:
                bv_replicate(1)
            for b in tail:
                b()
            if p == 0 and g == NG - 1:
                S.add("te", lambda e: e.matmul(
                    pSC[0:1, 0:2], s_ones_cf[:], s_gacc[:], start=True,
                    stop=True), reads=[("gacc",), ("ones3",)],
                    writes=[("pSC",)])
                S.add("dv", lambda e: e.tensor_scalar(
                    s_gs[0:1, 0:2], pSC[0:1, 0:2], 1.0 / (D * FF), EPS,
                    ALU.mult, ALU.add), reads=[("pSC",)], writes=[("gs",)])
                S.add("dv", lambda e: e.reciprocal(
                    s_gs[0:1, 0:2], s_gs[0:1, 0:2]),
                    reads=[("gs",)], writes=[("gs",)])
                S.add("te", lambda e: e.matmul(
                    pSD[:, 0:2], s_ones_rf[:], s_gs[0:1, 0:2], start=True,
                    stop=True), reads=[("gs",), ("ones2",)],
                    writes=[("pSD",)])
                S.add("dv", lambda e: e.tensor_copy(s_gs[:], pSD[:, 0:2]),
                      reads=[("pSD",)], writes=[("gs",)])

        if debug:
            dd = dbg_out("gs", [128, 2])
            dma("sp", dd, s_gs[:], "dbg_gs", reads=[("gs",)])
            dd2 = dbg_out("cneg", [128, 72])
            dma("sp", dd2, s_cneg[:, 0:72], "dbg_cneg", reads=[("cneg",)])

        ps.close()
        S.barrier("cd")

        cd = contextlib.ExitStack()
        s_woT = cd.enter_context(nc.sbuf_tensor("s_woT", [128, DC * D], BF16))
        s_w1qT = cd.enter_context(nc.sbuf_tensor("s_w1qT", [128, DC * FF], BF16))
        s_w2qT = cd.enter_context(nc.sbuf_tensor("s_w2qT", [128, FC * D], BF16))

        dma("gp", s_woT[:].rearrange("p (c d) -> p c d", c=DC),
            d_woT.rearrange("(c p) d -> p c d", p=128), "wo", writes=[("wo",)])
        s_otr = [cd.enter_context(nc.sbuf_tensor(f"s_otr{i}", [128, 1024],
                                                 BF16))
                 for i in range(4)]
        for t0 in range(4):
            S.add("sp", lambda e, t0=t0: e.dma_start(
                s_otr[t0][:].rearrange("p (a s) -> p a s", a=8),
                d_ots[:, t0, :, :]),
                reads=[("OTS", pair, 0) for pair in range(8)],
                writes=[("otr", t0)], dma="otr%d" % t0)

        cq = contextlib.ExitStack()
        s_wst = [cq.enter_context(nc.sbuf_tensor(f"s_wst{i}", [128, 2048], F32))
                 for i in range(4)]
        s_abs = cq.enter_context(nc.sbuf_tensor("s_abs", [128, 2048], F32))
        s_sg = cq.enter_context(nc.sbuf_tensor("s_sg", [128, 2048], BF16))
        s_p2 = cq.enter_context(nc.sbuf_tensor("s_p2", [128, 2048], BF16))
        s_m2 = s_sg

        def quant_w1(step, dst):
            b = step % 2
            src = d_w1T[step * 128:(step + 1) * 128, :]
            S.add("gp", lambda e, src=src, b=b: e.dma_start(s_wst[b][:], src),
                  writes=[("wst", b)], dma="wst%d" % b)
            S.add("ac", lambda e, b=b: e.activation(
                s_abs[:], s_wst[b][:], AF.Abs,
                scale=s_gs[:, 0:1]),
                reads=[("wst", b), ("gs",)], writes=[("abs",)])
            S.add("ac", lambda e, b=b: e.activation(
                s_sg[:], s_wst[b][:], AF.Sign),
                reads=[("wst", b)], writes=[("sg",)])
            S.add("dv", lambda e: e.tensor_scalar(
                s_abs[:], s_abs[:], 0.5, None, ALU.is_gt),
                reads=[("abs",)], writes=[("abs",)])
            S.add("dv", lambda e, dst=dst: e.tensor_mul(
                dst, s_abs[:], s_sg[:]),
                reads=[("abs",), ("sg",)], writes=[("w1q",)])

        def quant_w2(step, dst):
            b = 2 + step % 2
            for j in range(2):
                src = d_w2T[(2 * step + j) * 128:(2 * step + j + 1) * 128, :]
                S.add("sp", lambda e, src=src, b=b, j=j: e.dma_start(
                    s_wst[b][:, j * 1024:(j + 1) * 1024], src),
                    writes=[("wst", b)], dma="wst%d" % b)
            S.add("dv", lambda e, b=b: e.tensor_scalar(
                s_p2[:], s_wst[b][:], s_gs[:, 1:2], 0.5, ALU.mult, ALU.is_gt),
                reads=[("wst", b), ("gs",)], writes=[("p2",)])
            S.add("dv", lambda e, b=b: e.tensor_scalar(
                s_m2[:], s_wst[b][:], s_gs[:, 1:2], -0.5, ALU.mult, ALU.is_lt),
                reads=[("wst", b), ("gs",)], writes=[("sg",)])
            S.add("dv", lambda e, dst=dst: e.tensor_sub(
                dst, s_p2[:], s_m2[:]),
                reads=[("p2",), ("sg",)], writes=[("w2q",)])

        for step in range(16):
            rc, hh = divmod(step, 2)
            quant_w1(step, s_w1qT[:, rc * FF + hh * 2048:
                                  rc * FF + (hh + 1) * 2048])
            quant_w2(step, s_w2qT[:, 2 * step * D: 2 * (step + 1) * D])
        for t0 in range(2):
            pA = pBA if t0 == 0 else pBB
            pAk = "pBA" if t0 == 0 else "pBB"
            for half in range(2):
                S.add("te", lambda e, pA=pA, half=half: e.matmul(
                    pA[:, half * 512:(half + 1) * 512], s_ones_r[:],
                    s_bo[0:1, half * 512:(half + 1) * 512],
                    start=True, stop=False),
                    reads=[("bo",), ("ones",)], writes=[(pAk,)])
                for c in range(DC):
                    S.add("te", lambda e, pA=pA, half=half, c=c, t0=t0:
                          e.matmul(
                        pA[:, half * 512:(half + 1) * 512],
                        s_otr[t0][:, c * 128:(c + 1) * 128],
                        s_woT[:, c * D + half * 512: c * D + (half + 1) * 512],
                        start=False, stop=(c == DC - 1)),
                        reads=[("otr", t0), ("wo",)], writes=[(pAk,)])
        cq.close()
        S.barrier("cd2")

        s_hT = cd.enter_context(nc.sbuf_tensor("s_hT", [128, FC * 256], BF16))
        s_ln3T = cd.enter_context(nc.sbuf_tensor("s_ln3T", [128, DC * 512], BF16))
        s_x1 = [cd.enter_context(nc.sbuf_tensor(f"s_x1_{i}", [128, 1024],
                                                mybir.dt.float16))
                for i in range(4)]
        s_xb2 = [cd.enter_context(nc.sbuf_tensor(f"s_xb2_{i}", [128, 1024], F32))
                 for i in range(2)]
        s_outst = [cd.enter_context(nc.sbuf_tensor(f"s_outst{i}", [128, 1024], F32))
                   for i in range(1)]
        s_st12d = [cd.enter_context(nc.sbuf_tensor(f"s_st12d{i}", [128, 12], F32))
                   for i in range(4)]
        s_st2d = [cd.enter_context(nc.sbuf_tensor(f"s_st2d{i}", [128, 2], F32))
                  for i in range(4)]
        s_rstdd = [cd.enter_context(nc.sbuf_tensor(f"s_rstdd{i}", [128, 1], F32))
                   for i in range(4)]

        def ln_chain(b, srcs, out_ap, out_keys, out_srcs=None):
            rd = []
            for half, (ap, keys) in enumerate(srcs):
                S.add("dv", lambda e, b=b, ap=ap, half=half: e.bn_stats(
                    s_st12d[b][:, half * 6:half * 6 + 6], ap),
                    reads=list(keys), writes=[("dst12", b, half)])
                rd += list(keys)
            S.add("dv", lambda e, b=b: e.bn_aggr(s_st2d[b][:], s_st12d[b][:]),
                  reads=[("dst12", b, 0), ("dst12", b, 1)], writes=[("dst2", b)])
            S.add("ac", lambda e, b=b: e.activation(
                s_rstdd[b][:], s_st2d[b][:, 1:2], AF.Ln, bias=s_eps[:]),
                reads=[("dst2", b), ("eps",)], writes=[("drstd", b)])
            S.add("ac", lambda e, b=b: e.activation(
                s_rstdd[b][:], s_rstdd[b][:], AF.Exp, scale=-0.5),
                reads=[("drstd", b)], writes=[("drstd", b)])
            S.add("dv", lambda e, b=b: e.tensor_scalar(
                s_st2d[b][:, 0:1], s_st2d[b][:, 0:1], s_rstdd[b][:], -1.0,
                ALU.mult, ALU.mult),
                reads=[("dst2", b), ("drstd", b)], writes=[("dst2", b)])
            nsrc = out_srcs if out_srcs is not None else srcs
            for half, (ap, keys) in enumerate(nsrc):
                S.add("ac", lambda e, b=b, ap=ap, half=half, out_ap=out_ap:
                      e.activation(
                    out_ap[:, half * 512:(half + 1) * 512], ap, AF.Identity,
                    bias=s_st2d[b][:, 0:1], scale=s_rstdd[b][:]),
                    reads=list(keys) + [("dst2", b), ("drstd", b)],
                    writes=list(out_keys))

        w1keys = [("w1q",)]
        w2keys = [("w2q",)]
        NW = NT // 2

        def otr_load(t):
            if t >= NT:
                return
            S.add("sp", lambda e, t=t: e.dma_start(
                s_otr[t % 4][:].rearrange("p (a s) -> p a s", a=8),
                d_ots[:, t, :, :]),
                reads=[("OTS", pair, t // 4) for pair in range(8)],
                writes=[("otr", t % 4)], dma="otr%d" % (t % 4))

        def stage1a(t, mm=True):
            tl = t % 2
            x4 = t % 4
            pA = pBA if tl == 0 else pBB
            pAk = "pBA" if tl == 0 else "pBB"
            dma("gp", s_xb2[tl][:], d_x[t * 128:(t + 1) * 128, :], "xb2%d" % tl,
                writes=[("xb2", tl)])
            if not mm:
                return
            otr_load(t + 2)
            for half in range(2):
                S.add("te", lambda e, pA=pA, half=half: e.matmul(
                    pA[:, half * 512:(half + 1) * 512], s_ones_r[:],
                    s_bo[0:1, half * 512:(half + 1) * 512],
                    start=True, stop=False),
                    reads=[("bo",), ("ones",)], writes=[(pAk,)])
                for c in range(DC):
                    S.add("te", lambda e, pA=pA, half=half, c=c, x4=x4:
                          e.matmul(
                        pA[:, half * 512:(half + 1) * 512],
                        s_otr[x4][:, c * 128:(c + 1) * 128],
                        s_woT[:, c * D + half * 512: c * D + (half + 1) * 512],
                        start=False, stop=(c == DC - 1)),
                        reads=[("otr", x4), ("wo",)], writes=[(pAk,)])
        def stage1ln(t):
            tl = t % 2
            x4 = t % 4
            pA = pBA if tl == 0 else pBB
            pAk = "pBA" if tl == 0 else "pBB"
            ln_chain(tl, [(pA[:, 0:512], [(pAk,)]), (pA[:, 512:1024], [(pAk,)])],
                     s_x1[x4][:], [("x1", x4)])
            S.add("dv", lambda e, x4=x4, tl=tl: e.tensor_add(
                s_x1[x4][:], s_x1[x4][:], s_xb2[tl][:]),
                reads=[("x1", x4), ("xb2", tl)], writes=[("x1", x4)])
            ln_chain(2 + tl,
                     [(s_x1[x4][:, 0:512], [("x1", x4)]),
                      (s_x1[x4][:, 512:1024], [("x1", x4)])],
                     s_xb2[tl][:], [("xb2", tl)])

        def stage1b(t):
            tl = t % 2
            wpar = (t // 2) % 2
            for g2 in range(2):
                pp = pSA if g2 == 0 else pSB
                pk = "pSA" if g2 == 0 else "pSB"
                for c4 in range(4):
                    c = g2 * 4 + c4
                    S.add("te", lambda e, pp=pp, c4=c4, c=c, tl=tl: e.transpose(
                        pp[:, c4 * 128:(c4 + 1) * 128],
                        s_xb2[tl][:, c * 128:(c + 1) * 128], s_ident[:]),
                        reads=[("xb2", tl), ("ident",)], writes=[(pk,)])
                dst = s_ln3T[:].rearrange("p (c x) -> p c x", x=512)[
                    :, g2 * 4:(g2 + 1) * 4,
                    wpar * 256 + tl * 128: wpar * 256 + (tl + 1) * 128]
                src3 = pp[:].rearrange("p (c x) -> p c x", x=128)
                S.add("ac", lambda e, src3=src3, dst=dst: e.activation(
                    dst, src3, AF.Identity),
                    reads=[(pk,)], writes=[("l3", wpar, tl, g2)])

        def stage2a(w):
            wpar = w % 2
            l3keys = [("l3", wpar, tl, g2) for tl in range(2) for g2 in range(2)]
            for fc in range(FC):
                ph = pSA if fc % 2 == 0 else pSB
                phk = "pSA" if fc % 2 == 0 else "pSB"
                for c in range(DC):
                    S.add("te", lambda e, ph=ph, fc=fc, c=c, wpar=wpar:
                          e.matmul(
                        ph[:, 256:512],
                        s_w1qT[:, c * FF + fc * 128: c * FF + (fc + 1) * 128],
                        s_ln3T[:, c * 512 + wpar * 256: c * 512 + wpar * 256 + 256],
                        start=(c == 0), stop=(c == DC - 1)),
                        reads=l3keys + w1keys, writes=[(phk,)])
                S.add("ac", lambda e, ph=ph, fc=fc: e.activation(
                    s_hT[:, fc * 256:(fc + 1) * 256], ph[:, 256:512], AF.Relu,
                    bias=s_b1T[:, fc:fc + 1]),
                    reads=[(phk,), ("b1T",)], writes=[("hT", fc)])

        hkeys = [("hT", fc) for fc in range(FC)]

        def stage2b(w, tl):
            t = 2 * w + tl
            x4 = t % 4
            for half in range(2):
                pO = pSC if half == 0 else pSD
                pOk = "pSC" if half == 0 else "pSD"
                S.add("te", lambda e, pO=pO, half=half: e.matmul(
                    pO[:, 0:512], s_ones_r[:],
                    s_b2[0:1, half * 512:(half + 1) * 512],
                    start=True, stop=False),
                    reads=[("b2",), ("ones",)], writes=[(pOk,)])
                for fc in range(FC):
                    S.add("te", lambda e, pO=pO, half=half, fc=fc, tl=tl:
                          e.matmul(
                        pO[:, 0:512],
                        s_hT[:, fc * 256 + tl * 128: fc * 256 + tl * 128 + 128],
                        s_w2qT[:, fc * D + half * 512:
                               fc * D + (half + 1) * 512],
                        start=False, stop=(fc == FC - 1)),
                        reads=hkeys + w2keys, writes=[(pOk,)])
            ln_chain(2 + tl,
                     [(pSC[:, 0:512], [("pSC",)]), (pSD[:, 0:512], [("pSD",)])],
                     s_outst[0][:], [("os", 0)])
            S.add("dv", lambda e, tl=tl, x4=x4: e.tensor_add(
                s_outst[0][:], s_outst[0][:], s_x1[x4][:]),
                reads=[("os", 0), ("x1", x4)], writes=[("os", 0)])
            dma("sp", d_y[t * 128:(t + 1) * 128, :], s_outst[0][:],
                "yst0", reads=[("os", 0)], writes=[("y", t)])

        emitted = set()

        def s1a_emit(t):
            if t < NT and t not in emitted:
                emitted.add(t)
                stage1a(t, mm=(t >= 2))

        prevw = None
        for w in range(NW):
            s1a_emit(2 * w)
            s1a_emit(2 * w + 1)
            if prevw is not None:
                stage2a(prevw)
            stage1ln(2 * w)
            stage1ln(2 * w + 1)
            if prevw is not None:
                stage2b(prevw, 0)
            stage1b(2 * w)
            if prevw is not None:
                s1a_emit(2 * w + 2)
                stage2b(prevw, 1)
            stage1b(2 * w + 1)
            prevw = w
        stage2a(prevw)
        stage2b(prevw, 0)
        stage2b(prevw, 1)

        S.add("sp", lambda e: e.nop(), reads=[("y", t) for t in range(NT)])

        cd.close()
        S.emit(nc, block, st)
    return nc, dbg


def _host_inputs(L0_attn_x, Wq, bq, Wk, bk, Wv, bv, Wo, bo, W1, b1, W2, b2,
                 T=4096):
    f32 = np.float32
    import ml_dtypes
    tri = np.triu(np.ones((128, 128), np.float32)).astype(ml_dtypes.bfloat16)
    base = {
        "wqT": np.ascontiguousarray(np.asarray(Wq, f32).T),
        "wkT": np.ascontiguousarray(np.asarray(Wk, f32).T),
        "wvT": np.ascontiguousarray(np.asarray(Wv, f32).T),
        "woT": np.ascontiguousarray(np.asarray(Wo, f32).T),
        "w1Tp": np.ascontiguousarray(
            np.asarray(W1, f32).T.reshape(8, 128, 2, 2048)
            .transpose(0, 2, 1, 3).reshape(2048, 2048)),
        "w2T": np.ascontiguousarray(np.asarray(W2, f32).T),
        "bqT": np.ascontiguousarray(np.asarray(bq, f32).reshape(8, 128).T),
        "bkT": np.ascontiguousarray(np.asarray(bk, f32).reshape(8, 128).T),
        "b1T": np.ascontiguousarray(np.asarray(b1, f32).reshape(32, 128).T),
        "bqr": np.asarray(bq, f32).reshape(1, D),
        "bkr": np.asarray(bk, f32).reshape(1, D),
        "bvr": np.asarray(bv, f32).reshape(1, D),
        "bor": np.asarray(bo, f32).reshape(1, D),
        "b2r": np.asarray(b2, f32).reshape(1, D),
        "ident": np.eye(128, dtype=f32),
        "tri": tri,
    }
    xs = np.asarray(L0_attn_x, f32)
    return [dict(base, x=np.ascontiguousarray(xs[i])) for i in range(xs.shape[0])]


def kernel(L0_attn_x, L0_attn_Wq, L0_attn_bq, L0_attn_Wk, L0_attn_bk,
           L0_attn_Wv, L0_attn_bv, L0_attn_Wo, L0_attn_bo,
           L0_ffn_W1, L0_ffn_b1, L0_ffn_W2, L0_ffn_b2):
    in_maps = _host_inputs(L0_attn_x, L0_attn_Wq, L0_attn_bq, L0_attn_Wk,
                           L0_attn_bk, L0_attn_Wv, L0_attn_bv, L0_attn_Wo,
                           L0_attn_bo, L0_ffn_W1, L0_ffn_b1, L0_ffn_W2,
                           L0_ffn_b2)
    nc, _ = build_program(T=4096)
    res = run_bass_kernel_spmd(nc, in_maps, core_ids=list(range(8)))
    out = np.stack([np.asarray(r["y"]) for r in res.results], axis=0)
    return out.astype(np.float32)
